# version 14
# baseline (speedup 1.0000x reference)
import numpy as np
import ml_dtypes
import concourse.bass as bass
import concourse.mybir as mybir
from concourse.bass_utils import run_bass_kernel_spmd
from concourse.bass import AP
from contextlib import ExitStack

F32 = mybir.dt.float32
BF16 = mybir.dt.bfloat16
AF = mybir.ActivationFunctionType
ALU = mybir.AluOpType
AXX = mybir.AxisListType.X

T = 2048
C = 1024
NH = 16
DH = 64
NT = 16
HD = 2048
NE = 8
WS = 2048
JM = 1535
NEGBIG = -1.0e30

ENGS = ("sync", "scalar", "vector", "tensor", "gpsimd")


class Sem:
    __slots__ = ("h", "n", "name")

    def __init__(self, ctx, nc, name):
        self.h = ctx.enter_context(nc.semaphore(name))
        self.n = 0
        self.name = name


class Prog:
    def __init__(self):
        self.steps = {e: [] for e in ENGS}
        self.last_wait = {}

    def emit(self, eng, method, *args, waits=(), incs=(), **kw):
        wl = []
        for s, v in waits:
            if v <= 0:
                continue
            assert v <= s.n, f"wait on {s.name} for {v} but only {s.n} emitted"
            key = (eng, s.name)
            if self.last_wait.get(key, -1) < v:
                self.last_wait[key] = v
                wl.append((s.h, v))
        il = []
        for s, k in incs:
            s.n += k
            il.append((s.h, k))

        def fn(h, method=method, args=args, kw=kw, wl=wl, il=il):
            for sh, v in wl:
                h.wait_ge(sh, v)
            ins = getattr(h, method)(*args, **kw)
            for sh, k in il:
                ins.then_inc(sh, k)

        self.steps[eng].append(fn)


def build(debug=False):
    nc = bass.Bass("TRN2", target_bir_lowering=False, debug=False)

    x_in = nc.dram_tensor("x", [T, C], F32, kind="ExternalInput").ap()
    w_attn = nc.dram_tensor("w_attn", [C, 3 * C], F32, kind="ExternalInput").ap()
    w_proj = nc.dram_tensor("w_proj", [C, C], F32, kind="ExternalInput").ap()
    embkRT = nc.dram_tensor("embkRT", [DH, 1024], F32, kind="ExternalInput").ap()
    embvR = nc.dram_tensor("embvR", [1024, DH], F32, kind="ExternalInput").ap()
    wr_in = nc.dram_tensor("wr", [C, NE], F32, kind="ExternalInput").ap()
    br_in = nc.dram_tensor("br", [1, NE], F32, kind="ExternalInput").ap()
    w1_in = nc.dram_tensor("w1", [NE, C, HD], F32, kind="ExternalInput").ap()
    b1_in = nc.dram_tensor("b1", [NE, HD], F32, kind="ExternalInput").ap()
    w2_in = nc.dram_tensor("w2", [NE, HD, C], F32, kind="ExternalInput").ap()
    b2_in = nc.dram_tensor("b2", [NE, C], F32, kind="ExternalInput").ap()
    eye_bf = nc.dram_tensor("eye_bf", [128, 128], BF16, kind="ExternalInput").ap()
    eye64x2_bf = nc.dram_tensor("eye64x2_bf", [128, 64], BF16, kind="ExternalInput").ap()
    eye8_f = nc.dram_tensor("eye8_f", [8, 8], F32, kind="ExternalInput").ap()
    mask_bf = nc.dram_tensor("mask_bf", [128, 1024], BF16, kind="ExternalInput").ap()

    y_out = nc.dram_tensor("y", [T, C], F32, kind="ExternalOutput").ap()

    dbg = {}
    dkind = "ExternalOutput" if debug else "Internal"
    if debug:
        for nm, shp in (("dbg_xnT", [128, 8 * T]), ("dbg_qkv", [128, 6 * T]),
                        ("dbg_att", [128, T]), ("dbg_yT", [128, 8 * T]),
                        ("dbg_gates", [128, 2 * 128]), ("dbg_xn2T", [128, 8 * T]),
                        ("dbg_acc", [128, 16 * C])):
            dbg[nm] = nc.dram_tensor(nm, shp, F32, kind="ExternalOutput").ap()
    x_spill = nc.dram_tensor("x_spill", [T, C], F32, kind=dkind).ap()
    prev_h = [nc.dram_tensor(f"prevh{p}", [T, WS], BF16, kind=dkind).ap() for p in range(2)]
    bmat_h = [nc.dram_tensor(f"bmath{p}", [T, WS], BF16, kind=dkind).ap() for p in range(2)]

    P = Prog()
    EM = P.emit

    with ExitStack() as gctx:
        E = gctx.enter_context
        SM = lambda name: Sem(gctx, nc, name)

        eye_sb = E(nc.sbuf_tensor("eye_sb", [128, 128], BF16))
        eye64_sb = E(nc.sbuf_tensor("eye64_sb", [128, 64], BF16))
        eyef_sb = E(nc.sbuf_tensor("eyef_sb", [8, 8], F32))
        embkRT_sb = E(nc.sbuf_tensor("embkRT_sb", [128, 1024], BF16))
        e66_sb = E(nc.sbuf_tensor("e66_sb", [128, 8 * 66], BF16))
        ev_sb = E(nc.sbuf_tensor("ev_sb", [128, 64], BF16))
        ones_bf = E(nc.sbuf_tensor("ones_bf", [1, 128], BF16))
        ones_f = E(nc.sbuf_tensor("ones_f", [128, 64], F32))
        zero_bf = E(nc.sbuf_tensor("zero_bf", [128, 1024], BF16))
        negbig_bf = E(nc.sbuf_tensor("negbig_bf", [128, 512], BF16))
        xt_sb = E(nc.sbuf_tensor("xt_sb", [128, 2 * C], F32))
        xn_sb = E(nc.sbuf_tensor("xn_sb", [128, 2 * C], BF16))
        st_sb = E(nc.sbuf_tensor("st_sb", [128, 16], F32))
        plast_sb = E(nc.sbuf_tensor("plast_sb", [128, 16], F32))
        cst_sb = E(nc.sbuf_tensor("cst_sb", [128, 8], F32))

        s_const = SM("s_const")
        s_pe = SM("s_pe")
        s_act = SM("s_act")
        s_dve = SM("s_dve")
        s_x = [SM(f"s_x{k}") for k in range(2)]
        s_wq = [SM(f"s_wq{k}") for k in range(2)]
        s_prv = [SM(f"s_prv{k}") for k in range(4)]
        s_rr = [SM(f"s_rr{k}") for k in range(8)]
        s_bz = [SM(f"s_bz{k}") for k in range(2)]
        s_bw = [SM(f"s_bw{k}") for k in range(8)]
        s_bt = [SM(f"s_bt{k}") for k in range(8)]
        s_wp = SM("s_wp")
        s_pg = SM("s_pg")
        s_gp = SM("s_gp")
        s_spill = [SM(f"s_spill{k}") for k in range(2)]
        s_xp = [SM(f"s_xp{k}") for k in range(2)]
        s_xq = [SM(f"s_xq{k}") for k in range(2)]
        s_w1 = [SM(f"s_w1{k}") for k in range(2)]
        s_w2 = SM("s_w2")
        s_wrb = SM("s_wrb")
        s_out = [SM(f"s_out{k}") for k in range(2)]
        s_dbg = SM("s_dbg")

        EM("gpsimd", "dma_start", out=eye_sb[:], in_=eye_bf, incs=[(s_const, 16)])
        EM("gpsimd", "dma_start", out=eye64_sb[:], in_=eye64x2_bf, incs=[(s_const, 16)])
        EM("gpsimd", "dma_start", out=eyef_sb[:], in_=eye8_f, incs=[(s_const, 16)])
        EM("gpsimd", "dma_start", out=embkRT_sb[0:64, :], in_=embkRT, incs=[(s_const, 16)])
        EM("gpsimd", "dma_start", out=embkRT_sb[64:128, :], in_=embkRT, incs=[(s_const, 16)])
        for rt in range(8):
            EM("gpsimd", "dma_start", out=e66_sb[:, rt * 66:rt * 66 + 64],
               in_=embvR[rt * 128:(rt + 1) * 128, :], incs=[(s_const, 16)])
        EM("gpsimd", "dma_start", out=ev_sb[65:66, :], in_=embvR[0:1, :], incs=[(s_const, 16)])
        th = e66_sb[:].tensor
        EM("vector", "memset", AP(th, 64, [[8 * 66, 128], [66, 8], [1, 1]]), 0.0, incs=[(s_dve, 1)])
        EM("vector", "memset", AP(th, 65, [[8 * 66, 128], [66, 8], [1, 1]]), -1.0, incs=[(s_dve, 1)])
        EM("vector", "memset", ones_bf[:], 1.0, incs=[(s_dve, 1)])
        EM("vector", "memset", ones_f[:], 1.0, incs=[(s_dve, 1)])
        EM("vector", "memset", zero_bf[:], 0.0, incs=[(s_dve, 1)])
        EM("vector", "memset", ev_sb[0:65, :], 0.0, incs=[(s_dve, 1)])
        EM("vector", "memset", cst_sb[:, 0:1], 1e-5, incs=[(s_dve, 1)])
        EM("vector", "memset", cst_sb[:, 1:2], -0.5, incs=[(s_dve, 1)])
        EM("vector", "memset", cst_sb[:, 2:3], 0.125, incs=[(s_dve, 1)])
        EM("vector", "memset", cst_sb[:, 3:4], 1.0 / C, incs=[(s_dve, 1)])
        EM("vector", "memset", cst_sb[:, 4:5], -32.0, incs=[(s_dve, 1)])
        EM("vector", "memset", cst_sb[:, 5:6], 0.0, incs=[(s_dve, 1)])
        EM("vector", "memset", negbig_bf[:], NEGBIG, incs=[(s_dve, 1)])
        CONST_D = s_const.n
        MS_D = s_dve.n

        ln_prev_dve = [0]

        def emit_ln(xt_ap, xn_ap, first_waits):
            EM("scalar", "activation", xn_ap, xt_ap, AF.Copy,
               accum_out=st_sb[:, 0:1],
               waits=list(first_waits) + [(s_dve, ln_prev_dve[0])], incs=[(s_act, 1)])
            EM("scalar", "activation", xn_ap, xt_ap, AF.Square,
               accum_out=st_sb[:, 1:2], waits=[(s_act, s_act.n)], incs=[(s_act, 1)])
            EM("scalar", "activation", st_sb[:, 2:3], st_sb[:, 0:1], AF.Copy,
               scale=cst_sb[:, 3:4], waits=[(s_act, s_act.n)], incs=[(s_act, 1)])
            EM("scalar", "activation", st_sb[:, 3:4], st_sb[:, 1:2], AF.Copy,
               scale=cst_sb[:, 3:4], waits=[(s_act, s_act.n)], incs=[(s_act, 1)])
            EM("scalar", "activation", st_sb[:, 4:5], st_sb[:, 2:3], AF.Square,
               waits=[(s_act, s_act.n)], incs=[(s_act, 1)])
            a = s_act.n
            EM("vector", "tensor_tensor", st_sb[:, 5:6], st_sb[:, 3:4], st_sb[:, 4:5],
               ALU.subtract, waits=[(s_act, a)], incs=[(s_dve, 1)])
            d = s_dve.n
            EM("scalar", "activation", st_sb[:, 6:7], st_sb[:, 5:6], AF.Ln,
               bias=cst_sb[:, 0:1], waits=[(s_dve, d)], incs=[(s_act, 1)])
            EM("scalar", "activation", st_sb[:, 7:8], st_sb[:, 6:7], AF.Exp,
               scale=cst_sb[:, 1:2], waits=[(s_act, s_act.n)], incs=[(s_act, 1)])
            a = s_act.n
            EM("vector", "tensor_scalar", xn_ap, xt_ap, st_sb[:, 2:3], st_sb[:, 7:8],
               ALU.subtract, ALU.mult, waits=[(s_act, a)], incs=[(s_dve, 1)])
            ln_prev_dve[0] = s_dve.n
            return s_dve.n

        def emit_xpose(xn_tensor, xn_base, dstT_handle, t0, ps_tr, ps_tr_guard, extra_pe_waits):
            for cb in range(8):
                w = list(extra_pe_waits) + [(s_act, ps_tr_guard[0])] if cb == 0 else []
                EM("tensor", "transpose", ps_tr[:, cb * 128:(cb + 1) * 128],
                   xn_tensor[:, xn_base + cb * 128: xn_base + (cb + 1) * 128],
                   eye_sb[:], waits=w, incs=([(s_pe, 1)] if cb == 7 else []))
            pv = s_pe.n
            dst = AP(dstT_handle, t0, [[8 * T, 128], [T, 8], [1, 128]])
            EM("scalar", "copy", dst, ps_tr[:], waits=[(s_pe, pv)], incs=[(s_act, 1)])
            ps_tr_guard[0] = s_act.n
            return s_act.n

        with ExitStack() as actx:
            EA = actx.enter_context
            xnT = EA(nc.sbuf_tensor("xnT", [128, 8 * T], BF16))
            qkv_sb = EA(nc.sbuf_tensor("qkv_sb", [128, 6 * T], BF16))
            yT = EA(nc.sbuf_tensor("yT", [128, 8 * T], BF16))
            wqkv_sb = EA(nc.sbuf_tensor("wqkv_sb", [128, 6144], BF16))
            wproj_sb = EA(nc.sbuf_tensor("wproj_sb", [128, 8192], BF16))
            att_sb = EA(nc.sbuf_tensor("att_sb", [128, 4 * T], BF16))
            attT_sb = EA(nc.sbuf_tensor("attT_sb", [128, 1024], BF16))
            prev_sb = EA(nc.sbuf_tensor("prev_sb", [128, 2 * 1536], BF16))
            rel_sb = EA(nc.sbuf_tensor("rel_sb", [128, 4 * 2048], BF16))
            bt_sb = EA(nc.sbuf_tensor("bt_sb", [128, 8 * 512], BF16))
            vaug_sb = EA(nc.sbuf_tensor("vaug_sb", [128, 2 * 16 * 66], BF16))
            rep_sb = EA(nc.sbuf_tensor("rep_sb", [64, 2 * 512], F32))
            fin_sb = EA(nc.sbuf_tensor("fin_sb", [128, 1024], F32))
            finb_sb = EA(nc.sbuf_tensor("finb_sb", [128, 512], BF16))

            ps_s = [EA(nc.psum_tensor(f"ps_s{k}", [128, 512], F32)) for k in range(4)]
            ps_t = EA(nc.psum_tensor("ps_t", [128, 1024], BF16))
            ps_t2 = EA(nc.psum_tensor("ps_t2", [128, 1024], BF16))
            ps_y = [EA(nc.psum_tensor(f"ps_y{k}", [128, 512], F32)) for k in range(2)]

            vh = vaug_sb[:].tensor
            EM("vector", "memset", AP(vh, 64, [[2 * 16 * 66, 128], [66, 32], [1, 2]]), 1.0,
               incs=[(s_dve, 1)])
            MS_D2 = s_dve.n

            for p in range(2):
                for g16 in range(16):
                    EM("sync", "dma_start",
                       out=AP(prev_h[p].tensor, (g16 * 128) * WS + 1536,
                              [[WS, 128], [1, 512]]),
                       in_=negbig_bf[:],
                       waits=([(s_dve, MS_D)] if (p == 0 and g16 == 0) else []),
                       incs=[(s_pg, 16)])
            PG_D = s_pg.n

            for p in range(2):
                for i8 in range(8):
                    EM("sync", "dma_start",
                       out=AP(bmat_h[p].tensor, i8 * 128 * WS + 512,
                              [[WS, 128], [1, 1024]]),
                       in_=zero_bf[:],
                       waits=([(s_dve, MS_D)] if (p == 0 and i8 == 0) else []),
                       incs=[(s_bz[p], 16)])
            BZA = [s_bz[0].n, s_bz[1].n]

            pool_banks = ps_s
            pool_state = [[0, 0] for _ in range(4)]
            pool_ctr = [0]

            def pool_alloc():
                k = pool_ctr[0] % 4
                pool_ctr[0] += 1
                return k

            ps_t_guard = [0]
            ps_t_guard_d = [0]
            ps_t2_guard_d = [0]
            ps_s_guard = [(0, 0)] * 4
            ps_y_guard = [0, 0]
            aux_guard = [0]
            bt_guard = [0] * 8
            pending_fin = []
            pending_trav = []
            rel_guard = [0] * 8
            vaug_guard = [0, 0]
            bm_free = [0, 0]
            fin_pe_guard = [0]
            fin_war_dve = [0]
            fin_rec = [0]
            fin_fb = [0]
            plast_dve = [0]
            dbg_att_guard = [0]
            prev_free = [0, 0]
            att_tr_done = [0] * 4

            xt_free = [0, 0]
            xn_free = [0, 0]
            for i in range(NT):
                sl = i % 2
                xt_ap = xt_sb[:, sl * C:(sl + 1) * C]
                xn_base = sl * C
                xn_ap = xn_sb[:, sl * C:(sl + 1) * C]
                EM("sync", "dma_start", out=xt_ap, in_=x_in[i * 128:(i + 1) * 128, :],
                   waits=[(s_dve, xt_free[sl])], incs=[(s_x[sl], 16)])
                lv = s_x[sl].n
                xt_free[sl] = emit_ln(
                    xt_ap, xn_ap,
                    [(s_x[sl], lv), (s_const, CONST_D), (s_dve, MS_D2), (s_pe, xn_free[sl])])
                emit_xpose(xn_sb[:].tensor, xn_base, xnT[:].tensor, i * 128, ps_t,
                           ps_t_guard, [(s_dve, xt_free[sl])])
                xn_free[sl] = s_pe.n
            P1_ACT = s_act.n

            if debug:
                EM("gpsimd", "dma_start", out=dbg["dbg_xnT"], in_=xnT[:],
                   waits=[(s_act, P1_ACT)], incs=[(s_dbg, 16)])

            qkv_mm_end = [0] * 5

            def load_wqkv(g):
                gate = qkv_mm_end[g - 1] if g >= 1 else 0
                for k in range(8):
                    for j in range(3):
                        EM("gpsimd", "dma_start",
                           out=wqkv_sb[:, j * 2048 + k * 256: j * 2048 + (k + 1) * 256],
                           in_=w_attn[k * 128:(k + 1) * 128, j * C + g * 256: j * C + (g + 1) * 256],
                           waits=([(s_pe, gate)] if (k == 0 and j == 0) else []),
                           incs=[(s_wq[0], 16)])
                return s_wq[0].n

            wq_ready = [0] * 4
            wq_ready[0] = load_wqkv(0)

            for g in range(4):
                par = 0
                wbase = 0
                for jout in range(6):
                    j3, mh = jout // 2, jout % 2
                    for tc in range(4):
                        bk = pool_alloc()
                        bg = pool_state[bk]
                        EM("tensor", "matmul", pool_banks[bk][:],
                           wqkv_sb[:, wbase + j3 * 2048 + 0 * 256 + mh * 128:
                                   wbase + j3 * 2048 + 0 * 256 + mh * 128 + 128],
                           xnT[:, 0 * T + tc * 512: 0 * T + (tc + 1) * 512],
                           start=True, stop=False,
                           waits=[(s_wq[0], wq_ready[g]), (s_act, max(P1_ACT, bg[0])),
                                  (s_dve, bg[1])],
                           incs=[])
                        for k in range(1, 8):
                            EM("tensor", "matmul", pool_banks[bk][:],
                               wqkv_sb[:, wbase + j3 * 2048 + k * 256 + mh * 128:
                                       wbase + j3 * 2048 + k * 256 + mh * 128 + 128],
                               xnT[:, k * T + tc * 512: k * T + (tc + 1) * 512],
                               start=False, stop=(k == 7),
                               incs=([(s_pe, 1)] if k == 7 else []))
                        pv = s_pe.n
                        dst = qkv_sb[:, jout * T + tc * 512:jout * T + (tc + 1) * 512]
                        if j3 == 1:
                            EM("scalar", "activation", dst, pool_banks[bk][:], AF.Copy,
                               scale=cst_sb[:, 2:3],
                               waits=[(s_pe, pv)], incs=[(s_act, 1)])
                        else:
                            EM("scalar", "copy", dst, pool_banks[bk][:],
                               waits=[(s_pe, pv)], incs=[(s_act, 1)])
                        pool_state[bk][0] = s_act.n
                qkv_mm_end[g] = s_pe.n
                if g < 3:
                    wq_ready[g + 1] = load_wqkv(g + 1)
                QKV_ACT = s_act.n

                if debug and g == 0:
                    EM("gpsimd", "dma_start", out=dbg["dbg_qkv"], in_=qkv_sb[:],
                       waits=[(s_act, QKV_ACT)], incs=[(s_dbg, 16)])

                for hl in range(4):
                    h = 4 * g + hl
                    hpar = h % 2
                    qbase = (hl // 2) * T
                    poff = 64 * (hl % 2)
                    PRV = prev_h[hpar]
                    BM = bmat_h[hpar]

                    def qT(pa, pb, a, b):
                        return qkv_sb[pa:pb, 0 * T + qbase + a: 0 * T + qbase + b]

                    def kT(pa, pb, a, b):
                        return qkv_sb[pa:pb, 2 * T + qbase + a: 2 * T + qbase + b]

                    def vT(pa, pb, a, b):
                        return qkv_sb[pa:pb, 4 * T + qbase + a: 4 * T + qbase + b]

                    prv_w_val = [0, 0]
                    for i in range(NT):
                        psl = i % 2
                        cstart = max(0, 1408 - i * 128)
                        if pending_trav and i in (4, 8, 12):
                            pending_trav.pop(0)()
                        for u in range(2):
                            if u == 0 and cstart >= 1024:
                                continue
                            bk = pool_alloc()
                            bg = pool_state[bk]
                            EM("tensor", "matmul", pool_banks[bk][:],
                               qT(poff, poff + 64, i * 128, (i + 1) * 128),
                               embkRT_sb[poff:poff + 64, u * 512:(u + 1) * 512],
                               start=True, stop=True,
                               waits=[(s_act, bg[0]), (s_act, QKV_ACT),
                                      (s_const, CONST_D), (s_dve, bg[1])],
                               incs=[(s_pe, 1)])
                            pv = s_pe.n
                            if u == 0:
                                EM("vector", "tensor_copy", plast_sb[:, i:i + 1],
                                   pool_banks[bk][:, 0:1], waits=[(s_pe, pv)], incs=[(s_dve, 1)])
                                plast_dve[0] = s_dve.n
                                cs2 = max(cstart, 512)
                                EM("vector", "tensor_copy",
                                   prev_sb[:, psl * 1536 + cs2: psl * 1536 + 1024],
                                   pool_banks[bk][:, cs2 - 512: 512],
                                   waits=[(s_pe, pv), (s_prv[hpar * 2 + psl], prv_w_val[psl])],
                                   incs=[(s_dve, 1)])
                                pool_state[bk][1] = s_dve.n
                            else:
                                EM("scalar", "copy",
                                   prev_sb[:, psl * 1536 + 1024: psl * 1536 + 1536],
                                   pool_banks[bk][:],
                                   waits=[(s_pe, pv), (s_prv[hpar * 2 + psl], prv_w_val[psl])],
                                   incs=[(s_act, 1)])
                                pool_state[bk][0] = s_act.n
                        dvv = s_dve.n
                        if cstart < 512:
                            EM("scalar", "add",
                               prev_sb[:, psl * 1536 + cstart: psl * 1536 + 512],
                               zero_bf[:, 0:512 - cstart], plast_sb[:, i:i + 1],
                               waits=[(s_dve, dvv)], incs=[(s_act, 1)])
                        av = s_act.n
                        EM("sync", "dma_start",
                           out=AP(PRV.tensor, i * 128 * WS + cstart,
                                  [[WS, 128], [1, 1536 - cstart]]),
                           in_=prev_sb[:, psl * 1536 + cstart:(psl + 1) * 1536],
                           waits=([(s_act, av), (s_dve, dvv)] +
                                  ([(s_rr[hpar * 4 + q], prev_free[hpar]) for q in range(4)]
                                   if i == 0 else [])),
                           incs=[(s_prv[hpar * 2 + psl], 16)])
                        prv_w_val[psl] = s_prv[hpar * 2 + psl].n
                    PREV_D0 = s_prv[hpar * 2].n
                    PREV_D1 = s_prv[hpar * 2 + 1].n

                    strip_exp_act = [0] * NT
                    bw_val = [0] * 4
                    rel_issued = [False] * NT

                    def issue_rel(i):
                        t0 = i * 128
                        s_lo = max(0, t0 - 1408)
                        w_i = (i // 4 + 1) * 512 - s_lo
                        sl = i % 4
                        EM("sync", "dma_start",
                           out=rel_sb[:, sl * 2048: sl * 2048 + w_i],
                           in_=AP(PRV.tensor, t0 * WS + (JM - t0 + s_lo), [[WS - 1, 128], [1, w_i]]),
                           waits=[(s_prv[hpar * 2], PREV_D0), (s_prv[hpar * 2 + 1], PREV_D1),
                                  (s_dve, rel_guard[hpar * 4 + sl]), (s_pg, PG_D)],
                           incs=[(s_rr[hpar * 4 + sl], 16)])
                        rel_issued[i] = s_rr[hpar * 4 + sl].n

                    issue_rel(0)
                    issue_rel(1)
                    issue_rel(2)
                    for q4 in range(4):
                        for j in range(4):
                            st = q4 * 4 + j
                            EM("tensor", "transpose", ps_t[:, q4 * 256 + j * 64: q4 * 256 + (j + 1) * 64],
                               vT(poff, poff + 64, st * 128, (st + 1) * 128),
                               eye64_sb[poff:poff + 64, :],
                               waits=([(s_act, max(QKV_ACT, ps_t_guard[0])),
                                       (s_dve, ps_t_guard_d[0])] if j == 0 else []),
                               incs=([(s_pe, 1)] if j == 3 else []))
                        pv = s_pe.n
                        EM("scalar", "copy",
                           AP(vh, (hpar * 16 + q4 * 4) * 66, [[2 * 16 * 66, 128], [66, 4], [1, 64]]),
                           ps_t[:, q4 * 256:(q4 + 1) * 256],
                           waits=[(s_pe, max(pv, vaug_guard[hpar])), (s_dve, MS_D2)],
                           incs=[(s_act, 1)])
                        ps_t_guard[0] = s_act.n
                    VAUG_ACT = s_act.n

                    for i in range(NT):
                        gstrip = (g * 4 + hl) * NT + i
                        t0 = i * 128
                        nch = i // 4 + 1
                        s_lo = max(0, t0 - 1408)
                        ssl = i % 4
                        att_base = ssl * T
                        relv = rel_issued[i]
                        while pending_fin and pending_fin[0][0] <= gstrip:
                            pending_fin.pop(0)[1]()
                        chunk_act = [0] * nch
                        first = [True]

                        def fst():
                            if first[0]:
                                first[0] = False
                                return [(s_rr[hpar * 4 + ssl], relv), (s_const, CONST_D)]
                            return []

                        for ch in range(nch):
                            bk = pool_alloc()
                            bg = pool_state[bk]
                            EM("tensor", "matmul", pool_banks[bk][:],
                               qT(poff, poff + 64, t0, t0 + 128),
                               kT(poff, poff + 64, ch * 512, (ch + 1) * 512),
                               start=True, stop=True,
                               waits=[(s_act, bg[0]), (s_dve, bg[1])],
                               incs=[(s_pe, 1)])
                            qv = s_pe.n
                            a = max(ch * 512, s_lo)
                            b = (ch + 1) * 512
                            if b > a:
                                EM("vector", "tensor_tensor",
                                   pool_banks[bk][:, a - ch * 512: b - ch * 512],
                                   pool_banks[bk][:, a - ch * 512: b - ch * 512],
                                   rel_sb[:, ssl * 2048 + (a - s_lo): ssl * 2048 + (b - s_lo)],
                                   ALU.add, waits=fst() + [(s_pe, qv)], incs=[(s_dve, 1)])
                            if s_lo > ch * 512:
                                fb = min((ch + 1) * 512, s_lo)
                                EM("vector", "tensor_scalar",
                                   pool_banks[bk][:, 0: fb - ch * 512],
                                   pool_banks[bk][:, 0: fb - ch * 512],
                                   plast_sb[:, i:i + 1], None, ALU.add,
                                   waits=fst() + [(s_pe, qv)], incs=[(s_dve, 1)])
                            adv = s_dve.n
                            pool_state[bk][1] = adv
                            w = [(s_dve, adv)]
                            if ch == 0:
                                w += [(s_pe, att_tr_done[ssl]),
                                      (s_bw[hpar * 4 + ssl], bw_val[ssl]),
                                      (s_dbg, dbg_att_guard[0])]
                            EM("scalar", "activation",
                               att_sb[:, att_base + ch * 512: att_base + (ch + 1) * 512],
                               pool_banks[bk][:], AF.Exp, bias=cst_sb[:, 4:5], waits=w,
                               incs=[(s_act, 1)])
                            chunk_act[ch] = s_act.n
                            pool_state[bk][0] = s_act.n
                        sdv = s_dve.n
                        rel_guard[hpar * 4 + ssl] = sdv
                        if i + 3 < NT:
                            issue_rel(i + 3)
                        expv = s_act.n
                        strip_exp_act[i] = expv
                        w_i = t0 + 128 - s_lo
                        EM("sync", "dma_start",
                           out=AP(BM.tensor, t0 * WS + (JM - t0 + s_lo), [[WS - 1, 128], [1, w_i]]),
                           in_=att_sb[:, att_base + s_lo: att_base + t0 + 128],
                           waits=[(s_act, expv), (s_bz[hpar], BZA[hpar]),
                                  (s_pe, bm_free[hpar])],
                           incs=[(s_bw[hpar * 4 + ssl], 16)])
                        bw_val[ssl] = s_bw[hpar * 4 + ssl].n

                        def trav_closure(i=i, nch=nch, att_base=att_base, ssl=ssl,
                                         chunk_act=tuple(chunk_act),
                                         VAUG_ACT=VAUG_ACT, hpar=hpar):
                            ypar = (i // 4) % 2
                            nst = i + 1
                            ngr = (nst + 3) // 4
                            copy_dve = {}

                            def gspan(q4):
                                return q4 * 4, min(q4 * 4 + 4, nst)

                            def tbank(q4):
                                return ps_t if q4 % 2 == 0 else ps_t2

                            def emit_trans(q4):
                                j0, j1 = gspan(q4)
                                cov = chunk_act[min((j1 - 1) // 4, nch - 1)]
                                gd = copy_dve.get(
                                    q4 - 2,
                                    ps_t_guard_d[0] if q4 % 2 == 0 else ps_t2_guard_d[0])
                                for st in range(j0, j1):
                                    EM("tensor", "transpose",
                                       tbank(q4)[:, (st % 8) * 128: (st % 8 + 1) * 128],
                                       att_sb[:, att_base + st * 128: att_base + (st + 1) * 128],
                                       eye_sb[:],
                                       waits=([(s_act, max(cov, ps_t_guard[0])),
                                               (s_dve, gd)] if st == j0 else []),
                                       incs=([(s_pe, 1)] if st == j1 - 1 else []))
                                return s_pe.n

                            def emit_copy(q4, pv):
                                j0, j1 = gspan(q4)
                                EM("vector", "tensor_copy",
                                   attT_sb[:, (j0 % 8) * 128: (j0 % 8) * 128 + (j1 - j0) * 128],
                                   tbank(q4)[:, (j0 % 8) * 128: (j0 % 8) * 128 + (j1 - j0) * 128],
                                   waits=[(s_pe, pv)], incs=[(s_dve, 1)])
                                copy_dve[q4] = s_dve.n
                                if q4 % 2 == 0:
                                    ps_t_guard_d[0] = s_dve.n
                                else:
                                    ps_t2_guard_d[0] = s_dve.n

                            def emit_avs(q4):
                                j0, j1 = gspan(q4)
                                atv = copy_dve[q4]
                                for st in range(j0, j1):
                                    EM("tensor", "matmul",
                                       ps_y[ypar][0:66, (i % 4) * 128:(i % 4 + 1) * 128],
                                       vaug_sb[:, (hpar * 16 + st) * 66:(hpar * 16 + st) * 66 + 66],
                                       attT_sb[:, (st % 8) * 128:(st % 8 + 1) * 128],
                                       start=(i % 4 == 0 and st == 0), stop=False,
                                       skip_group_check=True,
                                       waits=([(s_dve, atv), (s_act, VAUG_ACT)] +
                                              ([(s_dve, ps_y_guard[ypar])]
                                               if (i % 4 == 0 and st == 0) else [])),
                                       incs=([(s_pe, 1)] if st == j1 - 1 else []))

                            pv0 = emit_trans(0)
                            if ngr == 1:
                                emit_copy(0, pv0)
                                emit_avs(0)
                            else:
                                emit_trans(1)
                                emit_copy(0, pv0)
                                for q4 in range(2, ngr):
                                    emit_trans(q4)
                                    emit_avs(q4 - 2)
                                    emit_copy(q4 - 1, s_pe.n)
                                emit_avs(ngr - 2)
                                emit_copy(ngr - 1, s_pe.n)
                                emit_avs(ngr - 1)
                            att_tr_done[ssl] = s_pe.n
                            vaug_guard[hpar] = s_pe.n

                        pending_trav.append(trav_closure)
                        if len(pending_trav) > 3:
                            pending_trav.pop(0)()

                        if debug and h == 0 and i == NT - 1:
                            EM("gpsimd", "dma_start", out=dbg["dbg_att"],
                               in_=att_sb[:, att_base:att_base + T],
                               waits=[(s_act, expv)], incs=[(s_dbg, 16)])
                            dbg_att_guard[0] = s_dbg.n

                        if i % 4 == 3:
                            tc = i // 4
                            ypar = (i // 4) % 2
                            bww = [(s_bw[hpar * 4 + q], s_bw[hpar * 4 + q].n) for q in range(4)]
                            btvals = []
                            for rt in range(8):
                                EM("sync", "dma_start",
                                   out=bt_sb[:, rt * 512:(rt + 1) * 512],
                                   in_=BM[tc * 512:(tc + 1) * 512, 512 + rt * 128: 640 + rt * 128],
                                   transpose=True,
                                   waits=bww + [(s_pe, bt_guard[rt]),
                                                (s_bz[hpar], BZA[hpar])],
                                   incs=[(s_bt[rt], 16)])
                                btvals.append(s_bt[rt].n)

                            def fin_closure(tc=tc, ypar=ypar, h=h, hpar=hpar,
                                            btvals=tuple(btvals)):
                                for rt in range(8):
                                    EM("tensor", "matmul", ps_y[ypar][0:66, :],
                                       e66_sb[:, rt * 66:(rt + 1) * 66],
                                       bt_sb[:, rt * 512:(rt + 1) * 512],
                                       start=False, stop=(rt == 7), skip_group_check=True,
                                       waits=[(s_bt[rt], btvals[rt]), (s_dve, MS_D)],
                                       incs=[(s_pe, 1)])
                                    bt_guard[rt] = s_pe.n
                                yrel_pe = s_pe.n
                                bm_free[hpar] = yrel_pe
                                EM("scalar", "copy", fin_sb[64:66, 0:512],
                                   ps_y[ypar][64:66, :],
                                   waits=[(s_pe, max(yrel_pe, fin_pe_guard[0])),
                                          (s_dve, fin_war_dve[0])],
                                   incs=[(s_act, 1)])
                                fav = s_act.n
                                EM("scalar", "copy", finb_sb[64:66, 0:512],
                                   ps_y[ypar][64:66, :], incs=[(s_act, 1)])
                                finb_av = s_act.n
                                EM("vector", "reciprocal", fin_sb[64:65, 512:1024],
                                   fin_sb[64:65, 0:512], waits=[(s_act, fav)],
                                   incs=[(s_dve, 1)])
                                fin_war_dve[0] = s_dve.n
                                fin_rec[0] = s_dve.n
                                fin_fb[0] = finb_av

                            def fin_closure_b(tc=tc, ypar=ypar, h=h,
                                              fin_rec=fin_rec, fin_fb=fin_fb):
                                EM("tensor", "matmul", ps_y[ypar][0:64, :],
                                   ev_sb[64:66, :], finb_sb[64:66, 0:512],
                                   start=False, stop=True, skip_group_check=True,
                                   waits=[(s_act, fin_fb[0]), (s_const, CONST_D)],
                                   incs=[(s_pe, 1)])
                                rbk = pool_alloc()
                                rbg = pool_state[rbk]
                                EM("tensor", "matmul", pool_banks[rbk][0:64, :],
                                   ones_f[64:65, :], fin_sb[64:65, 512:1024],
                                   start=True, stop=True,
                                   waits=[(s_act, rbg[0]), (s_dve, fin_rec[0]),
                                          (s_dve, rbg[1])],
                                   incs=[(s_pe, 1)])
                                fin_pe_guard[0] = s_pe.n
                                rep_pe = s_pe.n
                                EM("scalar", "copy",
                                   rep_sb[0:64, (tc % 2) * 512:(tc % 2 + 1) * 512],
                                   pool_banks[rbk][0:64, :], waits=[(s_pe, rep_pe)],
                                   incs=[(s_act, 1)])
                                pool_state[rbk][0] = s_act.n
                                rav = s_act.n
                                EM("vector", "tensor_tensor",
                                   yT[64 * (h % 2):64 * (h % 2) + 64,
                                      (h // 2) * T + tc * 512: (h // 2) * T + (tc + 1) * 512],
                                   ps_y[ypar][0:64, :],
                                   rep_sb[0:64, (tc % 2) * 512:(tc % 2 + 1) * 512],
                                   ALU.mult, waits=[(s_act, rav)], incs=[(s_dve, 1)])
                                ps_y_guard[ypar] = s_dve.n

                            pending_fin.append((gstrip + 4, fin_closure))
                            pending_fin.append((gstrip + 6, fin_closure_b))

                    prev_free[hpar] = max(s_rr[hpar * 4 + q].n for q in range(4))

            while pending_trav:
                pending_trav.pop(0)()
            while pending_fin:
                pending_fin.pop(0)[1]()
            YT_DVE = s_dve.n
            if debug:
                EM("gpsimd", "dma_start", out=dbg["dbg_yT"], in_=yT[:],
                   waits=[(s_dve, YT_DVE)], incs=[(s_dbg, 16)])

            for k in range(8):
                EM("gpsimd", "dma_start", out=wproj_sb[:, k * 1024:(k + 1) * 1024],
                   in_=w_proj[k * 128:(k + 1) * 128, :], incs=[(s_wp, 16)])
            WP_D = s_wp.n

            spill_val = [0, 0]
            for i in range(NT):
                sl = i % 2
                xt_ap = xt_sb[:, sl * C:(sl + 1) * C]
                EM("sync", "dma_start", out=xt_ap, in_=x_in[i * 128:(i + 1) * 128, :],
                   waits=[(s_spill[sl], spill_val[sl]), (s_dve, xt_free[sl])],
                   incs=[(s_x[sl], 16)])
                lv = s_x[sl].n
                for dc in range(2):
                    bk = pool_alloc()
                    psb = pool_banks[bk]
                    bg = pool_state[bk]
                    EM("tensor", "matmul", psb[:],
                       yT[:, 0 * T + i * 128: 0 * T + i * 128 + 128],
                       wproj_sb[:, 0 * 1024 + dc * 512: 0 * 1024 + (dc + 1) * 512],
                       start=True, stop=False,
                       waits=[(s_dve, max(YT_DVE, bg[1])), (s_act, bg[0]), (s_wp, WP_D)])
                    for k in range(1, 8):
                        EM("tensor", "matmul", psb[:],
                           yT[:, k * T + i * 128: k * T + i * 128 + 128],
                           wproj_sb[:, k * 1024 + dc * 512: k * 1024 + (dc + 1) * 512],
                           start=False, stop=(k == 7),
                           incs=([(s_pe, 1)] if k == 7 else []))
                    pv = s_pe.n
                    EM("vector", "tensor_tensor", xt_ap[:, dc * 512:(dc + 1) * 512],
                       psb[:], xt_ap[:, dc * 512:(dc + 1) * 512], ALU.add,
                       waits=[(s_pe, pv), (s_x[sl], lv)], incs=[(s_dve, 1)])
                    pool_state[bk][1] = s_dve.n
                dv = s_dve.n
                EM("sync", "dma_start", out=x_spill[i * 128:(i + 1) * 128, :], in_=xt_ap,
                   waits=[(s_dve, dv)], incs=[(s_spill[sl], 16)])
                spill_val[sl] = s_spill[sl].n
                xt_free[sl] = dv
            SPILL0 = s_spill[0].n
            SPILL1 = s_spill[1].n
            P4_PE = s_pe.n
            P4_DVE = s_dve.n
            P4_ACT = s_act.n

        with ExitStack() as fctx:
            EF = fctx.enter_context
            xn2T = EF(nc.sbuf_tensor("xn2T", [128, 8 * T], BF16))
            w1_sb = EF(nc.sbuf_tensor("w1_sb", [128, 8 * 2048], BF16))
            w2_sb = EF(nc.sbuf_tensor("w2_sb", [128, 16 * 1024], BF16))
            h_sb = EF(nc.sbuf_tensor("h_sb", [128, 16 * 512], BF16))
            acc_sb = EF(nc.sbuf_tensor("acc_sb", [128, 16 * 1024], BF16))
            wr_sb = EF(nc.sbuf_tensor("wr_sb", [128, 64], BF16))
            br_sb = EF(nc.sbuf_tensor("br_sb", [1, 8], BF16))
            b1c_sb = EF(nc.sbuf_tensor("b1c_sb", [128, 128], F32))
            rep_br = EF(nc.sbuf_tensor("rep_br", [128, 8], F32))
            b2_sb = EF(nc.sbuf_tensor("b2_sb", [8, 1024], BF16))
            p8_sb = EF(nc.sbuf_tensor("p8_sb", [128, 8], F32))
            g_sb = EF(nc.sbuf_tensor("g_sb", [128, 128], F32))
            gbf_sb = EF(nc.sbuf_tensor("gbf_sb", [128, 8], BF16))
            out_sb = EF(nc.sbuf_tensor("out_sb", [128, 2 * C], F32))
            gT_sb = EF(nc.sbuf_tensor("gT_sb", [8, 2048], BF16))

            ps_a = EF(nc.psum_tensor("ps_a", [128, 1024], BF16))
            ps_h = [EF(nc.psum_tensor(f"ps_h{k}", [128, 512], F32)) for k in range(2)]
            ps_v = [EF(nc.psum_tensor(f"ps_v{k}", [128, 512], F32)) for k in range(2)]
            ps_r = EF(nc.psum_tensor("ps_r", [128, 512], F32))

            BAR = [(s_pe, P4_PE), (s_dve, P4_DVE), (s_act, P4_ACT),
                   (s_spill[0], SPILL0), (s_spill[1], SPILL1),
                   (s_dbg, s_dbg.n)]

            for k in range(8):
                EM("gpsimd", "dma_start", out=wr_sb[:, k * 8:(k + 1) * 8],
                   in_=wr_in[k * 128:(k + 1) * 128, :],
                   waits=(BAR if k == 0 else []), incs=[(s_wrb, 16)])
            EM("gpsimd", "dma_start", out=br_sb[:], in_=br_in, incs=[(s_wrb, 16)])
            b1_8 = acc_sb[0:8, 0:4096].bitcast(F32)
            EM("gpsimd", "dma_start", out=b1_8, in_=b1_in,
               waits=BAR, incs=[(s_wrb, 16)])
            EM("gpsimd", "dma_start", out=b2_sb[:], in_=b2_in, incs=[(s_wrb, 16)])
            WRB_D = s_wrb.n
            for m in range(16):
                EM("tensor", "transpose", ps_r[0:128, 0:8],
                   b1_8[:, m * 128:(m + 1) * 128], eyef_sb[:],
                   waits=[(s_wrb, WRB_D)] + (BAR if m == 0 else []) +
                         ([(s_act, s_act.n)] if m > 0 else []),
                   incs=[(s_pe, 1)])
                pv = s_pe.n
                EM("scalar", "copy", b1c_sb[:, m * 8:(m + 1) * 8], ps_r[0:128, 0:8],
                   waits=[(s_pe, pv)], incs=[(s_act, 1)])
            EM("tensor", "matmul", ps_r[0:128, 0:8],
               ones_bf[0:1, :], br_sb[:], start=True, stop=True,
               waits=[(s_wrb, WRB_D), (s_act, s_act.n)], incs=[(s_pe, 1)])
            EM("scalar", "copy", rep_br[:], ps_r[0:128, 0:8],
               waits=[(s_pe, s_pe.n)], incs=[(s_act, 1)])
            B1C_ACT = s_act.n
            EM("vector", "memset", acc_sb[:], 0.0, waits=BAR + [(s_pe, s_pe.n)],
               incs=[(s_dve, 1)])
            ACCZ = s_dve.n

            def load_w1(e, gate_pe):
                for k in range(8):
                    EM("gpsimd", "dma_start",
                       out=w1_sb[:, k * 2048:(k + 1) * 2048],
                       in_=w1_in[e, k * 128:(k + 1) * 128, :],
                       waits=([(s_pe, gate_pe)] if k == 0 else []),
                       incs=[(s_w1[0], 16)])
                return s_w1[0].n

            def load_w2(e, gate_pe):
                for m in range(16):
                    EM("gpsimd", "dma_start",
                       out=w2_sb[:, m * 1024:(m + 1) * 1024],
                       in_=w2_in[e, m * 128:(m + 1) * 128, :],
                       waits=([(s_pe, gate_pe)] if m == 0 else []),
                       incs=[(s_w2, 16)])
                return s_w2.n

            ps_a_guard = [0]
            xp_free = [0, 0]
            xn_free2 = [0, 0]
            for i in range(NT):
                sl = i % 2
                xt_ap = xt_sb[:, sl * C:(sl + 1) * C]
                xn_base = sl * C
                xn_ap = xn_sb[:, sl * C:(sl + 1) * C]
                EM("sync", "dma_start", out=xt_ap, in_=x_spill[i * 128:(i + 1) * 128, :],
                   waits=[(s_spill[0], SPILL0), (s_spill[1], SPILL1),
                          (s_dve, xp_free[sl])], incs=[(s_xp[sl], 16)])
                lv = s_xp[sl].n
                xp_free[sl] = emit_ln(xt_ap, xn_ap, [(s_xp[sl], lv), (s_pe, xn_free2[sl])] + BAR)
                emit_xpose(xn_sb[:].tensor, xn_base, xn2T[:].tensor, i * 128, ps_a,
                           ps_a_guard, [(s_dve, xp_free[sl])] + (BAR if i < 2 else []))
                xn_free2[sl] = s_pe.n
            XP_FIN = [s_xp[0].n, s_xp[1].n]
            P5_ACT = s_act.n

            if debug:
                EM("gpsimd", "dma_start", out=dbg["dbg_xn2T"], in_=xn2T[:],
                   waits=[(s_act, P5_ACT)], incs=[(s_dbg, 16)])

            r_guard_act = B1C_ACT
            r_guard_dve = 0
            r_guard_gp = 0
            for i in range(NT):
                EM("tensor", "matmul", ps_r[0:128, 0:8],
                   xn2T[:, 0 * T + i * 128: 0 * T + i * 128 + 128], wr_sb[:, 0:8],
                   start=True, stop=False,
                   waits=[(s_act, max(P5_ACT, r_guard_act)), (s_wrb, WRB_D),
                          (s_dve, r_guard_dve)])
                for k in range(1, 8):
                    EM("tensor", "matmul", ps_r[0:128, 0:8],
                       xn2T[:, k * T + i * 128: k * T + i * 128 + 128],
                       wr_sb[:, k * 8:(k + 1) * 8], start=False, stop=(k == 7),
                       incs=([(s_pe, 1)] if k == 7 else []))
                pv = s_pe.n
                EM("vector", "tensor_tensor", ps_r[0:128, 0:8], ps_r[0:128, 0:8],
                   rep_br[:], ALU.add,
                   waits=[(s_pe, pv), (s_dve, r_guard_dve), (s_act, B1C_ACT),
                          (s_gp, r_guard_gp)],
                   incs=[(s_dve, 1)])
                dv = s_dve.n
                EM("scalar", "activation", p8_sb[:], ps_r[0:128, 0:8],
                   AF.Exp, accum_out=st_sb[:, 1:2],
                   waits=[(s_dve, dv)], incs=[(s_act, 1)])
                av = s_act.n
                r_guard_act = av
                EM("vector", "reciprocal", st_sb[:, 2:3], st_sb[:, 1:2],
                   waits=[(s_act, av)], incs=[(s_dve, 1)])
                EM("vector", "reduce_max", st_sb[:, 3:4], p8_sb[:], axis=AXX,
                   incs=[(s_dve, 1)])
                dv = s_dve.n
                EM("gpsimd", "tensor_tensor", st_sb[:, 4:5], st_sb[:, 3:4], st_sb[:, 2:3],
                   ALU.mult, waits=[(s_dve, dv)], incs=[(s_gp, 1)])
                gv = s_gp.n
                EM("vector", "tensor_scalar", g_sb[:, i * 8:(i + 1) * 8],
                   p8_sb[:], st_sb[:, 3:4], st_sb[:, 4:5],
                   ALU.is_ge, ALU.mult, waits=[(s_gp, gv)], incs=[(s_dve, 1)])
                dv = s_dve.n
                EM("gpsimd", "tensor_copy", gbf_sb[:], g_sb[:, i * 8:(i + 1) * 8],
                   waits=[(s_dve, dv)], incs=[(s_gp, 1)])
                r_guard_dve = s_dve.n
                r_guard_gp = s_gp.n
                EM("tensor", "transpose", ps_a[0:8, 0:128], gbf_sb[:], eye_sb[:],
                   waits=[(s_gp, r_guard_gp), (s_act, ps_a_guard[0])], incs=[(s_pe, 1)])
                pv = s_pe.n
                EM("scalar", "copy", gT_sb[0:8, i * 128:(i + 1) * 128], ps_a[0:8, 0:128],
                   waits=[(s_pe, pv)], incs=[(s_act, 1)])
                ps_a_guard[0] = s_act.n
                r_guard_act = s_act.n
            GATES_ACT = s_act.n
            GATES_DVE = s_dve.n

            if debug:
                EM("gpsimd", "dma_start", out=dbg["dbg_gates"][:, 0:128], in_=g_sb[:],
                   waits=[(s_dve, GATES_DVE)], incs=[(s_dbg, 16)])
                EM("gpsimd", "dma_start", out=dbg["dbg_gates"][:, 128:136], in_=p8_sb[:],
                   waits=[(s_dve, GATES_DVE)], incs=[(s_dbg, 16)])

            w1_rdy = [0, 0]
            w1_rdy[0] = load_w1(0, 0)
            h_last_pe = [0] * (NE + 1)
            y_last_pe = [0] * (NE + 1)
            h_evac_guard = 0
            ps_h_guard = [0, 0]
            ps_v_guard = [ACCZ, ACCZ]
            acc_dve = ACCZ
            for e in range(NE):
                par = 0
                w1v = w1_rdy[0]
                w2v = load_w2(e, y_last_pe[e])
                for c in range(4):
                    for m in range(16):
                        hp = ps_h[m % 2]
                        EM("tensor", "matmul", hp[:],
                           w1_sb[:, 0 * 2048 + m * 128: 0 * 2048 + (m + 1) * 128],
                           xn2T[:, 0 * T + c * 512: 0 * T + (c + 1) * 512],
                           start=True, stop=False,
                           waits=([(s_w1[0], w1v), (s_act, max(P5_ACT, B1C_ACT, ps_h_guard[m % 2]))]
                                  if m < 2 else [(s_act, ps_h_guard[m % 2])]))
                        for k in range(1, 8):
                            EM("tensor", "matmul", hp[:],
                               w1_sb[:, k * 2048 + m * 128: k * 2048 + (m + 1) * 128],
                               xn2T[:, k * T + c * 512: k * T + (c + 1) * 512],
                               start=False, stop=(k == 7),
                               incs=([(s_pe, 1)] if k == 7 else []))
                        pv = s_pe.n
                        EM("scalar", "activation", h_sb[:, m * 512:(m + 1) * 512], hp[:],
                           AF.Relu, bias=b1c_sb[:, m * 8 + e: m * 8 + e + 1],
                           waits=[(s_pe, max(pv, h_evac_guard))], incs=[(s_act, 1)])
                        ps_h_guard[m % 2] = s_act.n
                    h_act = s_act.n
                    if c == 3:
                        h_last_pe[e + 1] = s_pe.n
                        if e < NE - 1:
                            w1_rdy[0] = load_w1(e + 1, h_last_pe[e + 1])
                    for tt in range(4):
                        i = c * 4 + tt
                        for dc in range(2):
                            vp = ps_v[dc]
                            EM("tensor", "matmul", vp[:],
                               h_sb[:, 0 * 512 + tt * 128: 0 * 512 + tt * 128 + 128],
                               w2_sb[:, 0 * 1024 + dc * 512: 0 * 1024 + (dc + 1) * 512],
                               start=True, stop=False,
                               waits=[(s_act, h_act), (s_w2, w2v), (s_dve, ps_v_guard[dc])])
                            for m in range(1, 16):
                                EM("tensor", "matmul", vp[:],
                                   h_sb[:, m * 512 + tt * 128: m * 512 + tt * 128 + 128],
                                   w2_sb[:, m * 1024 + dc * 512: m * 1024 + (dc + 1) * 512],
                                   start=False, stop=(m == 15),
                                   incs=([(s_pe, 1)] if m == 15 else []))
                            pv = s_pe.n
                            EM("vector", "scalar_tensor_tensor",
                               acc_sb[:, i * 1024 + dc * 512: i * 1024 + (dc + 1) * 512],
                               vp[:], g_sb[:, i * 8 + e: i * 8 + e + 1],
                               acc_sb[:, i * 1024 + dc * 512: i * 1024 + (dc + 1) * 512],
                               ALU.mult, ALU.add,
                               waits=[(s_pe, pv), (s_dve, GATES_DVE)], incs=[(s_dve, 1)])
                            ps_v_guard[dc] = s_dve.n
                    acc_dve = s_dve.n
                    h_evac_guard = s_pe.n
                y_last_pe[e + 1] = s_pe.n
            ACC_DVE = s_dve.n

            if debug:
                EM("gpsimd", "dma_start", out=dbg["dbg_acc"], in_=acc_sb[:],
                   waits=[(s_dve, ACC_DVE)], incs=[(s_dbg, 16)])

            xq_free = [0, 0]
            out_val = [0, 0]
            for i in range(NT):
                sl = i % 2
                xt_ap = xt_sb[:, sl * C:(sl + 1) * C]
                o_ap = out_sb[:, sl * C:(sl + 1) * C]
                EM("sync", "dma_start", out=xt_ap, in_=x_spill[i * 128:(i + 1) * 128, :],
                   waits=[(s_spill[0], SPILL0), (s_spill[1], SPILL1),
                          (s_gp, xq_free[sl]), (s_dve, xp_free[sl]),
                          (s_xp[0], XP_FIN[0]), (s_xp[1], XP_FIN[1])],
                   incs=[(s_xq[sl], 16)])
                lv = s_xq[sl].n
                for dc in range(2):
                    EM("tensor", "matmul", ps_v[dc][:],
                       gT_sb[0:8, i * 128:(i + 1) * 128],
                       b2_sb[0:8, dc * 512:(dc + 1) * 512],
                       start=True, stop=True,
                       waits=[(s_act, GATES_ACT), (s_wrb, WRB_D), (s_dve, ps_v_guard[dc])],
                       incs=[(s_pe, 1)])
                gb_pe = s_pe.n
                for dc in range(2):
                    EM("vector", "scalar_tensor_tensor",
                       o_ap[:, dc * 512:(dc + 1) * 512], ps_v[dc][:], 1.0,
                       acc_sb[:, i * 1024 + dc * 512: i * 1024 + (dc + 1) * 512],
                       ALU.mult, ALU.add,
                       waits=[(s_pe, gb_pe), (s_dve, ACC_DVE),
                              (s_out[sl], out_val[sl]), (s_gp, xq_free[sl])],
                       incs=[(s_dve, 1)])
                dv = s_dve.n
                ps_v_guard[0] = dv
                ps_v_guard[1] = dv
                EM("gpsimd", "tensor_tensor", o_ap, o_ap, xt_ap, ALU.add,
                   waits=[(s_dve, dv), (s_xq[sl], lv)], incs=[(s_gp, 1)])
                gv = s_gp.n
                xq_free[sl] = gv
                EM("sync", "dma_start", out=y_out[i * 128:(i + 1) * 128, :], in_=o_ap,
                   waits=[(s_gp, gv)], incs=[(s_out[sl], 16)])
                out_val[sl] = s_out[sl].n
            OUT0 = s_out[0].n
            OUT1 = s_out[1].n

            def _final(h, OUT0=OUT0, OUT1=OUT1, DBG_D=s_dbg.n):
                h.wait_ge(s_out[0].h, OUT0)
                h.wait_ge(s_out[1].h, OUT1)
                if DBG_D:
                    h.wait_ge(s_dbg.h, DBG_D)

            P.steps["sync"].append(_final)

        with nc.Block() as block:
            @block.sync
            def _(sync):
                for fn in P.steps["sync"]:
                    fn(sync)

            @block.scalar
            def _(scalar):
                for fn in P.steps["scalar"]:
                    fn(scalar)

            @block.vector
            def _(vector):
                for fn in P.steps["vector"]:
                    fn(vector)

            @block.tensor
            def _(tensor):
                for fn in P.steps["tensor"]:
                    fn(tensor)

            @block.gpsimd
            def _(gpsimd):
                for fn in P.steps["gpsimd"]:
                    fn(gpsimd)

    return nc


def _host_consts():
    bf = ml_dtypes.bfloat16
    eye = np.eye(128, dtype=np.float32).astype(bf)
    eye64 = np.concatenate([np.eye(64, dtype=np.float32)] * 2, axis=0).astype(bf)
    eye8 = np.eye(8, dtype=np.float32)
    u = np.arange(1024)[None, :]
    p = np.arange(128)[:, None]
    mask = np.where(u <= p + 384, np.float32(0.0), np.float32(NEGBIG)).astype(bf)
    return eye, eye64, eye8, mask


_BUILD_CACHE = {}


def _get_nc(debug=False):
    key = bool(debug)
    if key not in _BUILD_CACHE:
        _BUILD_CACHE[key] = build(debug)
    return _BUILD_CACHE[key]


def kernel(L0_attn_x, L0_attn_w_attn, L0_attn_w_proj, L0_attn_embk, L0_attn_embv,
           L0_ffn_Wr, L0_ffn_br, L0_ffn_W1, L0_ffn_b1, L0_ffn_W2, L0_ffn_b2,
           _debug=False):
    x = np.ascontiguousarray(np.asarray(L0_attn_x, dtype=np.float32))
    w_attn = np.ascontiguousarray(np.asarray(L0_attn_w_attn, dtype=np.float32))
    w_proj = np.ascontiguousarray(np.asarray(L0_attn_w_proj, dtype=np.float32))
    embk = np.asarray(L0_attn_embk, dtype=np.float32)
    embv = np.asarray(L0_attn_embv, dtype=np.float32)
    wr = np.ascontiguousarray(np.asarray(L0_ffn_Wr, dtype=np.float32))
    br = np.ascontiguousarray(np.asarray(L0_ffn_br, dtype=np.float32).reshape(1, NE))
    w1 = np.ascontiguousarray(np.asarray(L0_ffn_W1, dtype=np.float32))
    b1 = np.ascontiguousarray(np.asarray(L0_ffn_b1, dtype=np.float32))
    w2 = np.ascontiguousarray(np.asarray(L0_ffn_W2, dtype=np.float32))
    b2 = np.ascontiguousarray(np.asarray(L0_ffn_b2, dtype=np.float32))

    embkRT = np.ascontiguousarray(embk[::-1, :].T)
    embvR = np.ascontiguousarray(embv[::-1, :])
    eye, eye64, eye8, mask = _host_consts()

    nc = _get_nc(_debug)
    common = dict(w_attn=w_attn, w_proj=w_proj, embkRT=embkRT, embvR=embvR,
                  wr=wr, br=br, w1=w1, b1=b1, w2=w2, b2=b2,
                  eye_bf=eye, eye64x2_bf=eye64, eye8_f=eye8, mask_bf=mask)
    in_maps = [dict(common, x=np.ascontiguousarray(x[b])) for b in range(8)]
    res = run_bass_kernel_spmd(nc, in_maps, core_ids=list(range(8)))
    out = np.stack([np.asarray(r["y"], dtype=np.float32) for r in res.results], axis=0)
    if _debug:
        return out, res
    return out


# version 15
# speedup vs baseline: 1.0427x; 1.0427x over previous
import numpy as np
import ml_dtypes
import concourse.bass as bass
import concourse.mybir as mybir
from concourse.bass_utils import run_bass_kernel_spmd
from concourse.bass import AP
from contextlib import ExitStack

F32 = mybir.dt.float32
BF16 = mybir.dt.bfloat16
AF = mybir.ActivationFunctionType
ALU = mybir.AluOpType
AXX = mybir.AxisListType.X

T = 2048
C = 1024
NH = 16
DH = 64
NT = 16
HD = 2048
NE = 8
WS = 2048
JM = 1535
NEGBIG = -1.0e30

ENGS = ("sync", "scalar", "vector", "tensor", "gpsimd")


class Sem:
    __slots__ = ("h", "n", "name")

    def __init__(self, ctx, nc, name):
        self.h = ctx.enter_context(nc.semaphore(name))
        self.n = 0
        self.name = name


class Prog:
    def __init__(self):
        self.steps = {e: [] for e in ENGS}
        self.last_wait = {}

    def emit(self, eng, method, *args, waits=(), incs=(), **kw):
        wl = []
        for s, v in waits:
            if v <= 0:
                continue
            assert v <= s.n, f"wait on {s.name} for {v} but only {s.n} emitted"
            key = (eng, s.name)
            if self.last_wait.get(key, -1) < v:
                self.last_wait[key] = v
                wl.append((s.h, v))
        il = []
        for s, k in incs:
            s.n += k
            il.append((s.h, k))

        def fn(h, method=method, args=args, kw=kw, wl=wl, il=il):
            for sh, v in wl:
                h.wait_ge(sh, v)
            ins = getattr(h, method)(*args, **kw)
            for sh, k in il:
                ins.then_inc(sh, k)

        self.steps[eng].append(fn)


def build(debug=False):
    nc = bass.Bass("TRN2", target_bir_lowering=False, debug=False)

    x_in = nc.dram_tensor("x", [T, C], F32, kind="ExternalInput").ap()
    w_attn = nc.dram_tensor("w_attn", [C, 3 * C], F32, kind="ExternalInput").ap()
    w_proj = nc.dram_tensor("w_proj", [C, C], F32, kind="ExternalInput").ap()
    embkRT = nc.dram_tensor("embkRT", [DH, 1024], F32, kind="ExternalInput").ap()
    embvR = nc.dram_tensor("embvR", [1024, DH], F32, kind="ExternalInput").ap()
    wr_in = nc.dram_tensor("wr", [C, NE], F32, kind="ExternalInput").ap()
    br_in = nc.dram_tensor("br", [1, NE], F32, kind="ExternalInput").ap()
    w1_in = nc.dram_tensor("w1", [NE, C, HD], F32, kind="ExternalInput").ap()
    b1_in = nc.dram_tensor("b1", [NE, HD], F32, kind="ExternalInput").ap()
    w2_in = nc.dram_tensor("w2", [NE, HD, C], F32, kind="ExternalInput").ap()
    b2_in = nc.dram_tensor("b2", [NE, C], F32, kind="ExternalInput").ap()
    eye_bf = nc.dram_tensor("eye_bf", [128, 128], BF16, kind="ExternalInput").ap()
    eye64x2_bf = nc.dram_tensor("eye64x2_bf", [128, 64], BF16, kind="ExternalInput").ap()
    eye8_f = nc.dram_tensor("eye8_f", [8, 8], F32, kind="ExternalInput").ap()
    mask_bf = nc.dram_tensor("mask_bf", [128, 1024], BF16, kind="ExternalInput").ap()

    y_out = nc.dram_tensor("y", [T, C], F32, kind="ExternalOutput").ap()

    dbg = {}
    dkind = "ExternalOutput" if debug else "Internal"
    if debug:
        for nm, shp in (("dbg_xnT", [128, 8 * T]), ("dbg_qkv", [128, 6 * T]),
                        ("dbg_att", [128, T]), ("dbg_yT", [128, 8 * T]),
                        ("dbg_gates", [128, 2 * 128]), ("dbg_xn2T", [128, 8 * T]),
                        ("dbg_acc", [128, 16 * C])):
            dbg[nm] = nc.dram_tensor(nm, shp, F32, kind="ExternalOutput").ap()
    x_spill = nc.dram_tensor("x_spill", [T, C], F32, kind=dkind).ap()
    prev_h = [nc.dram_tensor(f"prevh{p}", [T, WS], BF16, kind=dkind).ap() for p in range(2)]
    bmat_h = [nc.dram_tensor(f"bmath{p}", [T, WS], BF16, kind=dkind).ap() for p in range(2)]

    P = Prog()
    EM = P.emit

    with ExitStack() as gctx:
        E = gctx.enter_context
        SM = lambda name: Sem(gctx, nc, name)

        eye_sb = E(nc.sbuf_tensor("eye_sb", [128, 128], BF16))
        eye64_sb = E(nc.sbuf_tensor("eye64_sb", [128, 64], BF16))
        eyef_sb = E(nc.sbuf_tensor("eyef_sb", [8, 8], F32))
        embkRT_sb = E(nc.sbuf_tensor("embkRT_sb", [128, 1024], BF16))
        e66_sb = E(nc.sbuf_tensor("e66_sb", [128, 8 * 66], BF16))
        ev_sb = E(nc.sbuf_tensor("ev_sb", [128, 64], BF16))
        ones_bf = E(nc.sbuf_tensor("ones_bf", [1, 128], BF16))
        ones_f = E(nc.sbuf_tensor("ones_f", [128, 64], F32))
        zero_bf = E(nc.sbuf_tensor("zero_bf", [128, 1024], BF16))
        negbig_bf = E(nc.sbuf_tensor("negbig_bf", [128, 512], BF16))
        xt_sb = E(nc.sbuf_tensor("xt_sb", [128, 2 * C], F32))
        xn_sb = E(nc.sbuf_tensor("xn_sb", [128, 2 * C], BF16))
        st_sb = E(nc.sbuf_tensor("st_sb", [128, 16], F32))
        plast_sb = E(nc.sbuf_tensor("plast_sb", [128, 16], F32))
        cst_sb = E(nc.sbuf_tensor("cst_sb", [128, 8], F32))

        s_const = SM("s_const")
        s_pe = SM("s_pe")
        s_act = SM("s_act")
        s_dve = SM("s_dve")
        s_x = [SM(f"s_x{k}") for k in range(2)]
        s_wq = [SM(f"s_wq{k}") for k in range(2)]
        s_prv = [SM(f"s_prv{k}") for k in range(4)]
        s_rr = [SM(f"s_rr{k}") for k in range(8)]
        s_bz = [SM(f"s_bz{k}") for k in range(2)]
        s_bw = [SM(f"s_bw{k}") for k in range(8)]
        s_bt = [SM(f"s_bt{k}") for k in range(8)]
        s_wp = SM("s_wp")
        s_pg = SM("s_pg")
        s_gp = SM("s_gp")
        s_spill = [SM(f"s_spill{k}") for k in range(2)]
        s_xp = [SM(f"s_xp{k}") for k in range(2)]
        s_xq = [SM(f"s_xq{k}") for k in range(2)]
        s_w1 = [SM(f"s_w1{k}") for k in range(2)]
        s_w2 = SM("s_w2")
        s_wrb = SM("s_wrb")
        s_out = [SM(f"s_out{k}") for k in range(2)]
        s_dbg = SM("s_dbg")

        EM("gpsimd", "dma_start", out=eye_sb[:], in_=eye_bf, incs=[(s_const, 16)])
        EM("gpsimd", "dma_start", out=eye64_sb[:], in_=eye64x2_bf, incs=[(s_const, 16)])
        EM("gpsimd", "dma_start", out=eyef_sb[:], in_=eye8_f, incs=[(s_const, 16)])
        EM("gpsimd", "dma_start", out=embkRT_sb[0:64, :], in_=embkRT, incs=[(s_const, 16)])
        EM("gpsimd", "dma_start", out=embkRT_sb[64:128, :], in_=embkRT, incs=[(s_const, 16)])
        for rt in range(8):
            EM("gpsimd", "dma_start", out=e66_sb[:, rt * 66:rt * 66 + 64],
               in_=embvR[rt * 128:(rt + 1) * 128, :], incs=[(s_const, 16)])
        EM("gpsimd", "dma_start", out=ev_sb[65:66, :], in_=embvR[0:1, :], incs=[(s_const, 16)])
        th = e66_sb[:].tensor
        EM("vector", "memset", AP(th, 64, [[8 * 66, 128], [66, 8], [1, 1]]), 0.0, incs=[(s_dve, 1)])
        EM("vector", "memset", AP(th, 65, [[8 * 66, 128], [66, 8], [1, 1]]), -1.0, incs=[(s_dve, 1)])
        EM("vector", "memset", ones_bf[:], 1.0, incs=[(s_dve, 1)])
        EM("vector", "memset", ones_f[:], 1.0, incs=[(s_dve, 1)])
        EM("vector", "memset", zero_bf[:], 0.0, incs=[(s_dve, 1)])
        EM("vector", "memset", ev_sb[0:65, :], 0.0, incs=[(s_dve, 1)])
        EM("vector", "memset", cst_sb[:, 0:1], 1e-5, incs=[(s_dve, 1)])
        EM("vector", "memset", cst_sb[:, 1:2], -0.5, incs=[(s_dve, 1)])
        EM("vector", "memset", cst_sb[:, 2:3], 0.125, incs=[(s_dve, 1)])
        EM("vector", "memset", cst_sb[:, 3:4], 1.0 / C, incs=[(s_dve, 1)])
        EM("vector", "memset", cst_sb[:, 4:5], -32.0, incs=[(s_dve, 1)])
        EM("vector", "memset", cst_sb[:, 5:6], 0.0, incs=[(s_dve, 1)])
        EM("vector", "memset", negbig_bf[:], NEGBIG, incs=[(s_dve, 1)])
        CONST_D = s_const.n
        MS_D = s_dve.n

        ln_prev_dve = [0]

        def emit_ln(xt_ap, xn_ap, first_waits):
            EM("scalar", "activation", xn_ap, xt_ap, AF.Copy,
               accum_out=st_sb[:, 0:1],
               waits=list(first_waits) + [(s_dve, ln_prev_dve[0])], incs=[(s_act, 1)])
            EM("scalar", "activation", xn_ap, xt_ap, AF.Square,
               accum_out=st_sb[:, 1:2], waits=[(s_act, s_act.n)], incs=[(s_act, 1)])
            EM("scalar", "activation", st_sb[:, 2:3], st_sb[:, 0:1], AF.Copy,
               scale=cst_sb[:, 3:4], waits=[(s_act, s_act.n)], incs=[(s_act, 1)])
            EM("scalar", "activation", st_sb[:, 3:4], st_sb[:, 1:2], AF.Copy,
               scale=cst_sb[:, 3:4], waits=[(s_act, s_act.n)], incs=[(s_act, 1)])
            EM("scalar", "activation", st_sb[:, 4:5], st_sb[:, 2:3], AF.Square,
               waits=[(s_act, s_act.n)], incs=[(s_act, 1)])
            a = s_act.n
            EM("vector", "tensor_tensor", st_sb[:, 5:6], st_sb[:, 3:4], st_sb[:, 4:5],
               ALU.subtract, waits=[(s_act, a)], incs=[(s_dve, 1)])
            d = s_dve.n
            EM("scalar", "activation", st_sb[:, 6:7], st_sb[:, 5:6], AF.Ln,
               bias=cst_sb[:, 0:1], waits=[(s_dve, d)], incs=[(s_act, 1)])
            EM("scalar", "activation", st_sb[:, 7:8], st_sb[:, 6:7], AF.Exp,
               scale=cst_sb[:, 1:2], waits=[(s_act, s_act.n)], incs=[(s_act, 1)])
            a = s_act.n
            EM("vector", "tensor_scalar", xn_ap, xt_ap, st_sb[:, 2:3], st_sb[:, 7:8],
               ALU.subtract, ALU.mult, waits=[(s_act, a)], incs=[(s_dve, 1)])
            ln_prev_dve[0] = s_dve.n
            return s_dve.n

        def emit_xpose(xn_tensor, xn_base, dstT_handle, t0, ps_tr, ps_tr_guard, extra_pe_waits):
            for cb in range(8):
                w = list(extra_pe_waits) + [(s_act, ps_tr_guard[0])] if cb == 0 else []
                EM("tensor", "transpose", ps_tr[:, cb * 128:(cb + 1) * 128],
                   xn_tensor[:, xn_base + cb * 128: xn_base + (cb + 1) * 128],
                   eye_sb[:], waits=w, incs=([(s_pe, 1)] if cb == 7 else []))
            pv = s_pe.n
            dst = AP(dstT_handle, t0, [[8 * T, 128], [T, 8], [1, 128]])
            EM("scalar", "copy", dst, ps_tr[:], waits=[(s_pe, pv)], incs=[(s_act, 1)])
            ps_tr_guard[0] = s_act.n
            return s_act.n

        with ExitStack() as actx:
            EA = actx.enter_context
            xnT = EA(nc.sbuf_tensor("xnT", [128, 8 * T], BF16))
            qkv_sb = EA(nc.sbuf_tensor("qkv_sb", [128, 6 * T], BF16))
            yT = EA(nc.sbuf_tensor("yT", [128, 8 * T], BF16))
            wqkv_sb = EA(nc.sbuf_tensor("wqkv_sb", [128, 6144], BF16))
            wproj_sb = EA(nc.sbuf_tensor("wproj_sb", [128, 8192], BF16))
            att_sb = EA(nc.sbuf_tensor("att_sb", [128, 4 * T], BF16))
            attT_sb = EA(nc.sbuf_tensor("attT_sb", [128, 1024], BF16))
            prev_sb = EA(nc.sbuf_tensor("prev_sb", [128, 2 * 1536], BF16))
            rel_sb = EA(nc.sbuf_tensor("rel_sb", [128, 4 * 2048], BF16))
            bt_sb = EA(nc.sbuf_tensor("bt_sb", [128, 8 * 512], BF16))
            vaug_sb = EA(nc.sbuf_tensor("vaug_sb", [128, 2 * 16 * 66], BF16))
            rep_sb = EA(nc.sbuf_tensor("rep_sb", [64, 2 * 512], F32))
            fin_sb = EA(nc.sbuf_tensor("fin_sb", [128, 1024], F32))
            finb_sb = EA(nc.sbuf_tensor("finb_sb", [128, 512], BF16))

            ps_s = [EA(nc.psum_tensor(f"ps_s{k}", [128, 512], F32)) for k in range(4)]
            ps_t = EA(nc.psum_tensor("ps_t", [128, 1024], BF16))
            ps_t2 = EA(nc.psum_tensor("ps_t2", [128, 1024], BF16))
            ps_y = [EA(nc.psum_tensor(f"ps_y{k}", [128, 512], F32)) for k in range(2)]

            vh = vaug_sb[:].tensor
            EM("vector", "memset", AP(vh, 64, [[2 * 16 * 66, 128], [66, 32], [1, 2]]), 1.0,
               incs=[(s_dve, 1)])
            MS_D2 = s_dve.n

            for p in range(2):
                for g16 in range(16):
                    EM("sync", "dma_start",
                       out=AP(prev_h[p].tensor, (g16 * 128) * WS + 1536,
                              [[WS, 128], [1, 512]]),
                       in_=negbig_bf[:],
                       waits=([(s_dve, MS_D)] if (p == 0 and g16 == 0) else []),
                       incs=[(s_pg, 16)])
            PG_D = s_pg.n

            for p in range(2):
                for i8 in range(8):
                    EM("sync", "dma_start",
                       out=AP(bmat_h[p].tensor, i8 * 128 * WS + 512,
                              [[WS, 128], [1, 1024]]),
                       in_=zero_bf[:],
                       waits=([(s_dve, MS_D)] if (p == 0 and i8 == 0) else []),
                       incs=[(s_bz[p], 16)])
            BZA = [s_bz[0].n, s_bz[1].n]

            pool_banks = ps_s
            pool_state = [[0, 0] for _ in range(4)]
            pool_ctr = [0]

            def pool_alloc():
                k = pool_ctr[0] % 4
                pool_ctr[0] += 1
                return k

            ps_t_guard = [0]
            ps_t_guard_d = [0]
            ps_t2_guard_d = [0]
            ps_s_guard = [(0, 0)] * 4
            ps_y_guard = [0, 0]
            aux_guard = [0]
            bt_guard = [0] * 8
            pending_fin = []
            pending_trav = []
            rel_guard = [0] * 8
            vaug_guard = [0, 0]
            bm_free = [0, 0]
            fin_pe_guard = [0]
            fin_war_dve = [0]
            fin_rec = [0]
            fin_fb = [0]
            plast_dve = [0]
            dbg_att_guard = [0]
            prev_free = [0, 0]
            att_tr_done = [0] * 4

            xt_free = [0, 0]
            xn_free = [0, 0]
            for i in range(NT):
                sl = i % 2
                xt_ap = xt_sb[:, sl * C:(sl + 1) * C]
                xn_base = sl * C
                xn_ap = xn_sb[:, sl * C:(sl + 1) * C]
                EM("sync", "dma_start", out=xt_ap, in_=x_in[i * 128:(i + 1) * 128, :],
                   waits=[(s_dve, xt_free[sl])], incs=[(s_x[sl], 16)])
                lv = s_x[sl].n
                xt_free[sl] = emit_ln(
                    xt_ap, xn_ap,
                    [(s_x[sl], lv), (s_const, CONST_D), (s_dve, MS_D2), (s_pe, xn_free[sl])])
                emit_xpose(xn_sb[:].tensor, xn_base, xnT[:].tensor, i * 128, ps_t,
                           ps_t_guard, [(s_dve, xt_free[sl])])
                xn_free[sl] = s_pe.n
            P1_ACT = s_act.n

            if debug:
                EM("gpsimd", "dma_start", out=dbg["dbg_xnT"], in_=xnT[:],
                   waits=[(s_act, P1_ACT)], incs=[(s_dbg, 16)])

            qkv_mm_end = [0] * 5

            def load_wqkv(g):
                gate = qkv_mm_end[g - 1] if g >= 1 else 0
                for k in range(8):
                    for j in range(3):
                        EM("gpsimd", "dma_start",
                           out=wqkv_sb[:, j * 2048 + k * 256: j * 2048 + (k + 1) * 256],
                           in_=w_attn[k * 128:(k + 1) * 128, j * C + g * 256: j * C + (g + 1) * 256],
                           waits=([(s_pe, gate)] if (k == 0 and j == 0) else []),
                           incs=[(s_wq[0], 16)])
                return s_wq[0].n

            wq_ready = [0] * 4
            wq_ready[0] = load_wqkv(0)

            for g in range(4):
                par = 0
                wbase = 0
                for jout in range(6):
                    j3, mh = jout // 2, jout % 2
                    for tc in range(4):
                        bk = pool_alloc()
                        bg = pool_state[bk]
                        EM("tensor", "matmul", pool_banks[bk][:],
                           wqkv_sb[:, wbase + j3 * 2048 + 0 * 256 + mh * 128:
                                   wbase + j3 * 2048 + 0 * 256 + mh * 128 + 128],
                           xnT[:, 0 * T + tc * 512: 0 * T + (tc + 1) * 512],
                           start=True, stop=False,
                           waits=[(s_wq[0], wq_ready[g]), (s_act, max(P1_ACT, bg[0])),
                                  (s_dve, bg[1])],
                           incs=[])
                        for k in range(1, 8):
                            EM("tensor", "matmul", pool_banks[bk][:],
                               wqkv_sb[:, wbase + j3 * 2048 + k * 256 + mh * 128:
                                       wbase + j3 * 2048 + k * 256 + mh * 128 + 128],
                               xnT[:, k * T + tc * 512: k * T + (tc + 1) * 512],
                               start=False, stop=(k == 7),
                               incs=([(s_pe, 1)] if k == 7 else []))
                        pv = s_pe.n
                        dst = qkv_sb[:, jout * T + tc * 512:jout * T + (tc + 1) * 512]
                        if j3 == 1:
                            EM("scalar", "activation", dst, pool_banks[bk][:], AF.Copy,
                               scale=cst_sb[:, 2:3],
                               waits=[(s_pe, pv)], incs=[(s_act, 1)])
                        else:
                            EM("scalar", "copy", dst, pool_banks[bk][:],
                               waits=[(s_pe, pv)], incs=[(s_act, 1)])
                        pool_state[bk][0] = s_act.n
                qkv_mm_end[g] = s_pe.n
                if g < 3:
                    wq_ready[g + 1] = load_wqkv(g + 1)
                QKV_ACT = s_act.n

                if debug and g == 0:
                    EM("gpsimd", "dma_start", out=dbg["dbg_qkv"], in_=qkv_sb[:],
                       waits=[(s_act, QKV_ACT)], incs=[(s_dbg, 16)])

                for hl in range(4):
                    h = 4 * g + hl
                    hpar = h % 2
                    qbase = (hl // 2) * T
                    poff = 64 * (hl % 2)
                    PRV = prev_h[hpar]
                    BM = bmat_h[hpar]

                    def qT(pa, pb, a, b):
                        return qkv_sb[pa:pb, 0 * T + qbase + a: 0 * T + qbase + b]

                    def kT(pa, pb, a, b):
                        return qkv_sb[pa:pb, 2 * T + qbase + a: 2 * T + qbase + b]

                    def vT(pa, pb, a, b):
                        return qkv_sb[pa:pb, 4 * T + qbase + a: 4 * T + qbase + b]

                    prv_w_val = [0, 0]
                    for i in range(NT):
                        psl = i % 2
                        cstart = max(0, 1408 - i * 128)
                        if pending_trav and i in (4, 8, 12):
                            pending_trav.pop(0)()
                        for u in range(2):
                            if u == 0 and cstart >= 1024:
                                continue
                            bk = pool_alloc()
                            bg = pool_state[bk]
                            EM("tensor", "matmul", pool_banks[bk][:],
                               qT(poff, poff + 64, i * 128, (i + 1) * 128),
                               embkRT_sb[poff:poff + 64, u * 512:(u + 1) * 512],
                               start=True, stop=True,
                               waits=[(s_act, bg[0]), (s_act, QKV_ACT),
                                      (s_const, CONST_D), (s_dve, bg[1])],
                               incs=[(s_pe, 1)])
                            pv = s_pe.n
                            if u == 0:
                                EM("vector", "tensor_copy", plast_sb[:, i:i + 1],
                                   pool_banks[bk][:, 0:1], waits=[(s_pe, pv)], incs=[(s_dve, 1)])
                                plast_dve[0] = s_dve.n
                                cs2 = max(cstart, 512)
                                EM("vector", "tensor_copy",
                                   prev_sb[:, psl * 1536 + cs2: psl * 1536 + 1024],
                                   pool_banks[bk][:, cs2 - 512: 512],
                                   waits=[(s_pe, pv), (s_prv[hpar * 2 + psl], prv_w_val[psl])],
                                   incs=[(s_dve, 1)])
                                pool_state[bk][1] = s_dve.n
                            else:
                                EM("scalar", "copy",
                                   prev_sb[:, psl * 1536 + 1024: psl * 1536 + 1536],
                                   pool_banks[bk][:],
                                   waits=[(s_pe, pv), (s_prv[hpar * 2 + psl], prv_w_val[psl])],
                                   incs=[(s_act, 1)])
                                pool_state[bk][0] = s_act.n
                        dvv = s_dve.n
                        if cstart < 512:
                            EM("scalar", "add",
                               prev_sb[:, psl * 1536 + cstart: psl * 1536 + 512],
                               zero_bf[:, 0:512 - cstart], plast_sb[:, i:i + 1],
                               waits=[(s_dve, dvv)], incs=[(s_act, 1)])
                        av = s_act.n
                        EM("sync", "dma_start",
                           out=AP(PRV.tensor, i * 128 * WS + cstart,
                                  [[WS, 128], [1, 1536 - cstart]]),
                           in_=prev_sb[:, psl * 1536 + cstart:(psl + 1) * 1536],
                           waits=([(s_act, av), (s_dve, dvv)] +
                                  ([(s_rr[hpar * 4 + q], prev_free[hpar]) for q in range(4)]
                                   if i == 0 else [])),
                           incs=[(s_prv[hpar * 2 + psl], 16)])
                        prv_w_val[psl] = s_prv[hpar * 2 + psl].n
                    PREV_D0 = s_prv[hpar * 2].n
                    PREV_D1 = s_prv[hpar * 2 + 1].n

                    strip_exp_act = [0] * NT
                    bw_val = [0] * 4
                    rel_issued = [False] * NT

                    def issue_rel(i):
                        t0 = i * 128
                        s_lo = max(0, t0 - 1408)
                        w_i = (i // 4 + 1) * 512 - s_lo
                        sl = i % 4
                        EM("sync", "dma_start",
                           out=rel_sb[:, sl * 2048: sl * 2048 + w_i],
                           in_=AP(PRV.tensor, t0 * WS + (JM - t0 + s_lo), [[WS - 1, 128], [1, w_i]]),
                           waits=[(s_prv[hpar * 2], PREV_D0), (s_prv[hpar * 2 + 1], PREV_D1),
                                  (s_dve, rel_guard[hpar * 4 + sl]), (s_pg, PG_D)],
                           incs=[(s_rr[hpar * 4 + sl], 16)])
                        rel_issued[i] = s_rr[hpar * 4 + sl].n

                    issue_rel(0)
                    issue_rel(1)
                    issue_rel(2)
                    for q4 in range(4):
                        for j in range(4):
                            st = q4 * 4 + j
                            EM("tensor", "transpose", ps_t[:, q4 * 256 + j * 64: q4 * 256 + (j + 1) * 64],
                               vT(poff, poff + 64, st * 128, (st + 1) * 128),
                               eye64_sb[poff:poff + 64, :],
                               waits=([(s_act, max(QKV_ACT, ps_t_guard[0])),
                                       (s_dve, ps_t_guard_d[0])] if j == 0 else []),
                               incs=([(s_pe, 1)] if j == 3 else []))
                        pv = s_pe.n
                        EM("scalar", "copy",
                           AP(vh, (hpar * 16 + q4 * 4) * 66, [[2 * 16 * 66, 128], [66, 4], [1, 64]]),
                           ps_t[:, q4 * 256:(q4 + 1) * 256],
                           waits=[(s_pe, max(pv, vaug_guard[hpar])), (s_dve, MS_D2)],
                           incs=[(s_act, 1)])
                        ps_t_guard[0] = s_act.n
                    VAUG_ACT = s_act.n

                    for i in range(NT):
                        gstrip = (g * 4 + hl) * NT + i
                        t0 = i * 128
                        nch = i // 4 + 1
                        s_lo = max(0, t0 - 1408)
                        ssl = i % 4
                        att_base = ssl * T
                        relv = rel_issued[i]
                        while pending_fin and pending_fin[0][0] <= gstrip:
                            pending_fin.pop(0)[1]()
                        chunk_act = [0] * nch
                        first = [True]

                        def fst():
                            if first[0]:
                                first[0] = False
                                return [(s_rr[hpar * 4 + ssl], relv), (s_const, CONST_D)]
                            return []

                        for ch in range(nch):
                            bk = pool_alloc()
                            bg = pool_state[bk]
                            EM("tensor", "matmul", pool_banks[bk][:],
                               qT(poff, poff + 64, t0, t0 + 128),
                               kT(poff, poff + 64, ch * 512, (ch + 1) * 512),
                               start=True, stop=True,
                               waits=[(s_act, bg[0]), (s_dve, bg[1])],
                               incs=[(s_pe, 1)])
                            qv = s_pe.n
                            a = max(ch * 512, s_lo)
                            b = (ch + 1) * 512
                            if b > a:
                                EM("vector", "tensor_tensor",
                                   pool_banks[bk][:, a - ch * 512: b - ch * 512],
                                   pool_banks[bk][:, a - ch * 512: b - ch * 512],
                                   rel_sb[:, ssl * 2048 + (a - s_lo): ssl * 2048 + (b - s_lo)],
                                   ALU.add, waits=fst() + [(s_pe, qv)], incs=[(s_dve, 1)])
                            if s_lo > ch * 512:
                                fb = min((ch + 1) * 512, s_lo)
                                EM("vector", "tensor_scalar",
                                   pool_banks[bk][:, 0: fb - ch * 512],
                                   pool_banks[bk][:, 0: fb - ch * 512],
                                   plast_sb[:, i:i + 1], None, ALU.add,
                                   waits=fst() + [(s_pe, qv)], incs=[(s_dve, 1)])
                            adv = s_dve.n
                            pool_state[bk][1] = adv
                            w = [(s_dve, adv)]
                            if ch == 0:
                                w += [(s_pe, att_tr_done[ssl]),
                                      (s_bw[hpar * 4 + ssl], bw_val[ssl]),
                                      (s_dbg, dbg_att_guard[0])]
                            EM("scalar", "activation",
                               att_sb[:, att_base + ch * 512: att_base + (ch + 1) * 512],
                               pool_banks[bk][:], AF.Exp, bias=cst_sb[:, 4:5], waits=w,
                               incs=[(s_act, 1)])
                            chunk_act[ch] = s_act.n
                            pool_state[bk][0] = s_act.n
                        sdv = s_dve.n
                        rel_guard[hpar * 4 + ssl] = sdv
                        if i + 3 < NT:
                            issue_rel(i + 3)
                        expv = s_act.n
                        strip_exp_act[i] = expv
                        w_i = t0 + 128 - s_lo
                        EM("sync", "dma_start",
                           out=AP(BM.tensor, t0 * WS + (JM - t0 + s_lo), [[WS - 1, 128], [1, w_i]]),
                           in_=att_sb[:, att_base + s_lo: att_base + t0 + 128],
                           waits=[(s_act, expv), (s_bz[hpar], BZA[hpar]),
                                  (s_pe, bm_free[hpar])],
                           incs=[(s_bw[hpar * 4 + ssl], 16)])
                        bw_val[ssl] = s_bw[hpar * 4 + ssl].n

                        def trav_closure(i=i, nch=nch, att_base=att_base, ssl=ssl,
                                         chunk_act=tuple(chunk_act),
                                         VAUG_ACT=VAUG_ACT, hpar=hpar):
                            ypar = (i // 4) % 2
                            nst = i + 1
                            ngr = (nst + 3) // 4
                            copy_dve = {}

                            def gspan(q4):
                                return q4 * 4, min(q4 * 4 + 4, nst)

                            def tbank(q4):
                                return ps_t if q4 % 2 == 0 else ps_t2

                            def emit_trans(q4):
                                j0, j1 = gspan(q4)
                                cov = chunk_act[min((j1 - 1) // 4, nch - 1)]
                                gd = copy_dve.get(
                                    q4 - 2,
                                    ps_t_guard_d[0] if q4 % 2 == 0 else ps_t2_guard_d[0])
                                for st in range(j0, j1):
                                    EM("tensor", "transpose",
                                       tbank(q4)[:, (st % 8) * 128: (st % 8 + 1) * 128],
                                       att_sb[:, att_base + st * 128: att_base + (st + 1) * 128],
                                       eye_sb[:],
                                       waits=([(s_act, max(cov, ps_t_guard[0])),
                                               (s_dve, gd)] if st == j0 else []),
                                       incs=([(s_pe, 1)] if st == j1 - 1 else []))
                                return s_pe.n

                            def emit_copy(q4, pv):
                                j0, j1 = gspan(q4)
                                EM("vector", "tensor_copy",
                                   attT_sb[:, (j0 % 8) * 128: (j0 % 8) * 128 + (j1 - j0) * 128],
                                   tbank(q4)[:, (j0 % 8) * 128: (j0 % 8) * 128 + (j1 - j0) * 128],
                                   waits=[(s_pe, pv)], incs=[(s_dve, 1)])
                                copy_dve[q4] = s_dve.n
                                if q4 % 2 == 0:
                                    ps_t_guard_d[0] = s_dve.n
                                else:
                                    ps_t2_guard_d[0] = s_dve.n

                            def emit_avs(q4):
                                j0, j1 = gspan(q4)
                                atv = copy_dve[q4]
                                for st in range(j0, j1):
                                    EM("tensor", "matmul",
                                       ps_y[ypar][0:66, (i % 4) * 128:(i % 4 + 1) * 128],
                                       vaug_sb[:, (hpar * 16 + st) * 66:(hpar * 16 + st) * 66 + 66],
                                       attT_sb[:, (st % 8) * 128:(st % 8 + 1) * 128],
                                       start=(i % 4 == 0 and st == 0), stop=False,
                                       skip_group_check=True,
                                       waits=([(s_dve, atv), (s_act, VAUG_ACT)] +
                                              ([(s_dve, ps_y_guard[ypar])]
                                               if (i % 4 == 0 and st == 0) else [])),
                                       incs=([(s_pe, 1)] if st == j1 - 1 else []))

                            pvs = {}
                            avs_pe = {}

                            def emit_avs2(q):
                                emit_avs(q)
                                avs_pe[q] = s_pe.n

                            pvs[0] = emit_trans(0)
                            if ngr == 1:
                                emit_copy(0, pvs[0])
                                emit_avs2(0)
                            else:
                                pvs[1] = emit_trans(1)
                                emit_copy(0, pvs[0])
                                for q4 in range(2, ngr):
                                    pvs[q4] = emit_trans(q4)
                                    emit_avs2(q4 - 2)
                                    emit_copy(q4 - 1,
                                              max(pvs[q4 - 1], avs_pe.get(q4 - 3, 0)))
                                emit_avs2(ngr - 2)
                                emit_copy(ngr - 1,
                                          max(pvs[ngr - 1], avs_pe.get(ngr - 3, 0)))
                                emit_avs2(ngr - 1)
                            att_tr_done[ssl] = s_pe.n
                            vaug_guard[hpar] = s_pe.n

                        pending_trav.append(trav_closure)
                        if len(pending_trav) > 3:
                            pending_trav.pop(0)()

                        if debug and h == 0 and i == NT - 1:
                            EM("gpsimd", "dma_start", out=dbg["dbg_att"],
                               in_=att_sb[:, att_base:att_base + T],
                               waits=[(s_act, expv)], incs=[(s_dbg, 16)])
                            dbg_att_guard[0] = s_dbg.n

                        if i % 4 == 3:
                            tc = i // 4
                            ypar = (i // 4) % 2
                            bww = [(s_bw[hpar * 4 + q], s_bw[hpar * 4 + q].n) for q in range(4)]
                            btvals = []
                            for rt in range(8):
                                EM("sync", "dma_start",
                                   out=bt_sb[:, rt * 512:(rt + 1) * 512],
                                   in_=BM[tc * 512:(tc + 1) * 512, 512 + rt * 128: 640 + rt * 128],
                                   transpose=True,
                                   waits=bww + [(s_pe, bt_guard[rt]),
                                                (s_bz[hpar], BZA[hpar])],
                                   incs=[(s_bt[rt], 16)])
                                btvals.append(s_bt[rt].n)

                            def fin_closure(tc=tc, ypar=ypar, h=h, hpar=hpar,
                                            btvals=tuple(btvals)):
                                for rt in range(8):
                                    EM("tensor", "matmul", ps_y[ypar][0:66, :],
                                       e66_sb[:, rt * 66:(rt + 1) * 66],
                                       bt_sb[:, rt * 512:(rt + 1) * 512],
                                       start=False, stop=(rt == 7), skip_group_check=True,
                                       waits=[(s_bt[rt], btvals[rt]), (s_dve, MS_D)],
                                       incs=[(s_pe, 1)])
                                    bt_guard[rt] = s_pe.n
                                yrel_pe = s_pe.n
                                bm_free[hpar] = yrel_pe
                                EM("scalar", "copy", fin_sb[64:66, 0:512],
                                   ps_y[ypar][64:66, :],
                                   waits=[(s_pe, max(yrel_pe, fin_pe_guard[0])),
                                          (s_dve, fin_war_dve[0])],
                                   incs=[(s_act, 1)])
                                fav = s_act.n
                                EM("scalar", "copy", finb_sb[64:66, 0:512],
                                   ps_y[ypar][64:66, :], incs=[(s_act, 1)])
                                finb_av = s_act.n
                                EM("vector", "reciprocal", fin_sb[64:65, 512:1024],
                                   fin_sb[64:65, 0:512], waits=[(s_act, fav)],
                                   incs=[(s_dve, 1)])
                                fin_war_dve[0] = s_dve.n
                                fin_rec[0] = s_dve.n
                                fin_fb[0] = finb_av

                            def fin_closure_b(tc=tc, ypar=ypar, h=h,
                                              fin_rec=fin_rec, fin_fb=fin_fb):
                                EM("tensor", "matmul", ps_y[ypar][0:64, :],
                                   ev_sb[64:66, :], finb_sb[64:66, 0:512],
                                   start=False, stop=True, skip_group_check=True,
                                   waits=[(s_act, fin_fb[0]), (s_const, CONST_D)],
                                   incs=[(s_pe, 1)])
                                rbk = pool_alloc()
                                rbg = pool_state[rbk]
                                EM("tensor", "matmul", pool_banks[rbk][0:64, :],
                                   ones_f[64:65, :], fin_sb[64:65, 512:1024],
                                   start=True, stop=True,
                                   waits=[(s_act, rbg[0]), (s_dve, fin_rec[0]),
                                          (s_dve, rbg[1])],
                                   incs=[(s_pe, 1)])
                                fin_pe_guard[0] = s_pe.n
                                rep_pe = s_pe.n
                                EM("scalar", "copy",
                                   rep_sb[0:64, (tc % 2) * 512:(tc % 2 + 1) * 512],
                                   pool_banks[rbk][0:64, :], waits=[(s_pe, rep_pe)],
                                   incs=[(s_act, 1)])
                                pool_state[rbk][0] = s_act.n
                                rav = s_act.n
                                EM("vector", "tensor_tensor",
                                   yT[64 * (h % 2):64 * (h % 2) + 64,
                                      (h // 2) * T + tc * 512: (h // 2) * T + (tc + 1) * 512],
                                   ps_y[ypar][0:64, :],
                                   rep_sb[0:64, (tc % 2) * 512:(tc % 2 + 1) * 512],
                                   ALU.mult, waits=[(s_act, rav)], incs=[(s_dve, 1)])
                                ps_y_guard[ypar] = s_dve.n

                            pending_fin.append((gstrip + 4, fin_closure))
                            pending_fin.append((gstrip + 6, fin_closure_b))

                    prev_free[hpar] = max(s_rr[hpar * 4 + q].n for q in range(4))

            while pending_trav:
                pending_trav.pop(0)()
            while pending_fin:
                pending_fin.pop(0)[1]()
            YT_DVE = s_dve.n
            if debug:
                EM("gpsimd", "dma_start", out=dbg["dbg_yT"], in_=yT[:],
                   waits=[(s_dve, YT_DVE)], incs=[(s_dbg, 16)])

            for k in range(8):
                EM("gpsimd", "dma_start", out=wproj_sb[:, k * 1024:(k + 1) * 1024],
                   in_=w_proj[k * 128:(k + 1) * 128, :], incs=[(s_wp, 16)])
            WP_D = s_wp.n

            spill_val = [0, 0]
            for i in range(NT):
                sl = i % 2
                xt_ap = xt_sb[:, sl * C:(sl + 1) * C]
                EM("sync", "dma_start", out=xt_ap, in_=x_in[i * 128:(i + 1) * 128, :],
                   waits=[(s_spill[sl], spill_val[sl]), (s_dve, xt_free[sl])],
                   incs=[(s_x[sl], 16)])
                lv = s_x[sl].n
                for dc in range(2):
                    bk = pool_alloc()
                    psb = pool_banks[bk]
                    bg = pool_state[bk]
                    EM("tensor", "matmul", psb[:],
                       yT[:, 0 * T + i * 128: 0 * T + i * 128 + 128],
                       wproj_sb[:, 0 * 1024 + dc * 512: 0 * 1024 + (dc + 1) * 512],
                       start=True, stop=False,
                       waits=[(s_dve, max(YT_DVE, bg[1])), (s_act, bg[0]), (s_wp, WP_D)])
                    for k in range(1, 8):
                        EM("tensor", "matmul", psb[:],
                           yT[:, k * T + i * 128: k * T + i * 128 + 128],
                           wproj_sb[:, k * 1024 + dc * 512: k * 1024 + (dc + 1) * 512],
                           start=False, stop=(k == 7),
                           incs=([(s_pe, 1)] if k == 7 else []))
                    pv = s_pe.n
                    EM("vector", "tensor_tensor", xt_ap[:, dc * 512:(dc + 1) * 512],
                       psb[:], xt_ap[:, dc * 512:(dc + 1) * 512], ALU.add,
                       waits=[(s_pe, pv), (s_x[sl], lv)], incs=[(s_dve, 1)])
                    pool_state[bk][1] = s_dve.n
                dv = s_dve.n
                EM("sync", "dma_start", out=x_spill[i * 128:(i + 1) * 128, :], in_=xt_ap,
                   waits=[(s_dve, dv)], incs=[(s_spill[sl], 16)])
                spill_val[sl] = s_spill[sl].n
                xt_free[sl] = dv
            SPILL0 = s_spill[0].n
            SPILL1 = s_spill[1].n
            P4_PE = s_pe.n
            P4_DVE = s_dve.n
            P4_ACT = s_act.n

        with ExitStack() as fctx:
            EF = fctx.enter_context
            xn2T = EF(nc.sbuf_tensor("xn2T", [128, 8 * T], BF16))
            w1_sb = EF(nc.sbuf_tensor("w1_sb", [128, 8 * 2048], BF16))
            w2_sb = EF(nc.sbuf_tensor("w2_sb", [128, 16 * 1024], BF16))
            h_sb = EF(nc.sbuf_tensor("h_sb", [128, 16 * 512], BF16))
            acc_sb = EF(nc.sbuf_tensor("acc_sb", [128, 16 * 1024], BF16))
            wr_sb = EF(nc.sbuf_tensor("wr_sb", [128, 64], BF16))
            br_sb = EF(nc.sbuf_tensor("br_sb", [1, 8], BF16))
            b1c_sb = EF(nc.sbuf_tensor("b1c_sb", [128, 128], F32))
            rep_br = EF(nc.sbuf_tensor("rep_br", [128, 8], F32))
            b2_sb = EF(nc.sbuf_tensor("b2_sb", [8, 1024], BF16))
            p8_sb = EF(nc.sbuf_tensor("p8_sb", [128, 8], F32))
            g_sb = EF(nc.sbuf_tensor("g_sb", [128, 128], F32))
            gbf_sb = EF(nc.sbuf_tensor("gbf_sb", [128, 8], BF16))
            out_sb = EF(nc.sbuf_tensor("out_sb", [128, 2 * C], F32))
            gT_sb = EF(nc.sbuf_tensor("gT_sb", [8, 2048], BF16))

            ps_a = EF(nc.psum_tensor("ps_a", [128, 1024], BF16))
            ps_h = [EF(nc.psum_tensor(f"ps_h{k}", [128, 512], F32)) for k in range(2)]
            ps_v = [EF(nc.psum_tensor(f"ps_v{k}", [128, 512], F32)) for k in range(2)]
            ps_r = EF(nc.psum_tensor("ps_r", [128, 512], F32))

            BAR = [(s_pe, P4_PE), (s_dve, P4_DVE), (s_act, P4_ACT),
                   (s_spill[0], SPILL0), (s_spill[1], SPILL1),
                   (s_dbg, s_dbg.n)]

            for k in range(8):
                EM("gpsimd", "dma_start", out=wr_sb[:, k * 8:(k + 1) * 8],
                   in_=wr_in[k * 128:(k + 1) * 128, :],
                   waits=(BAR if k == 0 else []), incs=[(s_wrb, 16)])
            EM("gpsimd", "dma_start", out=br_sb[:], in_=br_in, incs=[(s_wrb, 16)])
            b1_8 = acc_sb[0:8, 0:4096].bitcast(F32)
            EM("gpsimd", "dma_start", out=b1_8, in_=b1_in,
               waits=BAR, incs=[(s_wrb, 16)])
            EM("gpsimd", "dma_start", out=b2_sb[:], in_=b2_in, incs=[(s_wrb, 16)])
            WRB_D = s_wrb.n
            for m in range(16):
                EM("tensor", "transpose", ps_r[0:128, 0:8],
                   b1_8[:, m * 128:(m + 1) * 128], eyef_sb[:],
                   waits=[(s_wrb, WRB_D)] + (BAR if m == 0 else []) +
                         ([(s_act, s_act.n)] if m > 0 else []),
                   incs=[(s_pe, 1)])
                pv = s_pe.n
                EM("scalar", "copy", b1c_sb[:, m * 8:(m + 1) * 8], ps_r[0:128, 0:8],
                   waits=[(s_pe, pv)], incs=[(s_act, 1)])
            EM("tensor", "matmul", ps_r[0:128, 0:8],
               ones_bf[0:1, :], br_sb[:], start=True, stop=True,
               waits=[(s_wrb, WRB_D), (s_act, s_act.n)], incs=[(s_pe, 1)])
            EM("scalar", "copy", rep_br[:], ps_r[0:128, 0:8],
               waits=[(s_pe, s_pe.n)], incs=[(s_act, 1)])
            B1C_ACT = s_act.n
            EM("vector", "memset", acc_sb[:], 0.0, waits=BAR + [(s_pe, s_pe.n)],
               incs=[(s_dve, 1)])
            ACCZ = s_dve.n

            def load_w1(e, gate_pe):
                for k in range(8):
                    EM("gpsimd", "dma_start",
                       out=w1_sb[:, k * 2048:(k + 1) * 2048],
                       in_=w1_in[e, k * 128:(k + 1) * 128, :],
                       waits=([(s_pe, gate_pe)] if k == 0 else []),
                       incs=[(s_w1[0], 16)])
                return s_w1[0].n

            def load_w2(e, gate_pe):
                for m in range(16):
                    EM("gpsimd", "dma_start",
                       out=w2_sb[:, m * 1024:(m + 1) * 1024],
                       in_=w2_in[e, m * 128:(m + 1) * 128, :],
                       waits=([(s_pe, gate_pe)] if m == 0 else []),
                       incs=[(s_w2, 16)])
                return s_w2.n

            ps_a_guard = [0]
            xp_free = [0, 0]
            xn_free2 = [0, 0]
            for i in range(NT):
                sl = i % 2
                xt_ap = xt_sb[:, sl * C:(sl + 1) * C]
                xn_base = sl * C
                xn_ap = xn_sb[:, sl * C:(sl + 1) * C]
                EM("sync", "dma_start", out=xt_ap, in_=x_spill[i * 128:(i + 1) * 128, :],
                   waits=[(s_spill[0], SPILL0), (s_spill[1], SPILL1),
                          (s_dve, xp_free[sl])], incs=[(s_xp[sl], 16)])
                lv = s_xp[sl].n
                xp_free[sl] = emit_ln(xt_ap, xn_ap, [(s_xp[sl], lv), (s_pe, xn_free2[sl])] + BAR)
                emit_xpose(xn_sb[:].tensor, xn_base, xn2T[:].tensor, i * 128, ps_a,
                           ps_a_guard, [(s_dve, xp_free[sl])] + (BAR if i < 2 else []))
                xn_free2[sl] = s_pe.n
            XP_FIN = [s_xp[0].n, s_xp[1].n]
            P5_ACT = s_act.n

            if debug:
                EM("gpsimd", "dma_start", out=dbg["dbg_xn2T"], in_=xn2T[:],
                   waits=[(s_act, P5_ACT)], incs=[(s_dbg, 16)])

            r_guard_act = B1C_ACT
            r_guard_dve = 0
            r_guard_gp = 0
            for i in range(NT):
                EM("tensor", "matmul", ps_r[0:128, 0:8],
                   xn2T[:, 0 * T + i * 128: 0 * T + i * 128 + 128], wr_sb[:, 0:8],
                   start=True, stop=False,
                   waits=[(s_act, max(P5_ACT, r_guard_act)), (s_wrb, WRB_D),
                          (s_dve, r_guard_dve)])
                for k in range(1, 8):
                    EM("tensor", "matmul", ps_r[0:128, 0:8],
                       xn2T[:, k * T + i * 128: k * T + i * 128 + 128],
                       wr_sb[:, k * 8:(k + 1) * 8], start=False, stop=(k == 7),
                       incs=([(s_pe, 1)] if k == 7 else []))
                pv = s_pe.n
                EM("vector", "tensor_tensor", ps_r[0:128, 0:8], ps_r[0:128, 0:8],
                   rep_br[:], ALU.add,
                   waits=[(s_pe, pv), (s_dve, r_guard_dve), (s_act, B1C_ACT),
                          (s_gp, r_guard_gp)],
                   incs=[(s_dve, 1)])
                dv = s_dve.n
                EM("scalar", "activation", p8_sb[:], ps_r[0:128, 0:8],
                   AF.Exp, accum_out=st_sb[:, 1:2],
                   waits=[(s_dve, dv)], incs=[(s_act, 1)])
                av = s_act.n
                r_guard_act = av
                EM("vector", "reciprocal", st_sb[:, 2:3], st_sb[:, 1:2],
                   waits=[(s_act, av)], incs=[(s_dve, 1)])
                EM("vector", "reduce_max", st_sb[:, 3:4], p8_sb[:], axis=AXX,
                   incs=[(s_dve, 1)])
                dv = s_dve.n
                EM("gpsimd", "tensor_tensor", st_sb[:, 4:5], st_sb[:, 3:4], st_sb[:, 2:3],
                   ALU.mult, waits=[(s_dve, dv)], incs=[(s_gp, 1)])
                gv = s_gp.n
                EM("vector", "tensor_scalar", g_sb[:, i * 8:(i + 1) * 8],
                   p8_sb[:], st_sb[:, 3:4], st_sb[:, 4:5],
                   ALU.is_ge, ALU.mult, waits=[(s_gp, gv)], incs=[(s_dve, 1)])
                dv = s_dve.n
                EM("gpsimd", "tensor_copy", gbf_sb[:], g_sb[:, i * 8:(i + 1) * 8],
                   waits=[(s_dve, dv)], incs=[(s_gp, 1)])
                r_guard_dve = s_dve.n
                r_guard_gp = s_gp.n
                EM("tensor", "transpose", ps_a[0:8, 0:128], gbf_sb[:], eye_sb[:],
                   waits=[(s_gp, r_guard_gp), (s_act, ps_a_guard[0])], incs=[(s_pe, 1)])
                pv = s_pe.n
                EM("scalar", "copy", gT_sb[0:8, i * 128:(i + 1) * 128], ps_a[0:8, 0:128],
                   waits=[(s_pe, pv)], incs=[(s_act, 1)])
                ps_a_guard[0] = s_act.n
                r_guard_act = s_act.n
            GATES_ACT = s_act.n
            GATES_DVE = s_dve.n

            if debug:
                EM("gpsimd", "dma_start", out=dbg["dbg_gates"][:, 0:128], in_=g_sb[:],
                   waits=[(s_dve, GATES_DVE)], incs=[(s_dbg, 16)])
                EM("gpsimd", "dma_start", out=dbg["dbg_gates"][:, 128:136], in_=p8_sb[:],
                   waits=[(s_dve, GATES_DVE)], incs=[(s_dbg, 16)])

            w1_rdy = [0, 0]
            w1_rdy[0] = load_w1(0, 0)
            h_last_pe = [0] * (NE + 1)
            y_last_pe = [0] * (NE + 1)
            h_evac_guard = 0
            ps_h_guard = [0, 0]
            ps_v_guard = [ACCZ, ACCZ]
            acc_dve = ACCZ
            for e in range(NE):
                par = 0
                w1v = w1_rdy[0]
                w2v = load_w2(e, y_last_pe[e])
                for c in range(4):
                    for m in range(16):
                        hp = ps_h[m % 2]
                        EM("tensor", "matmul", hp[:],
                           w1_sb[:, 0 * 2048 + m * 128: 0 * 2048 + (m + 1) * 128],
                           xn2T[:, 0 * T + c * 512: 0 * T + (c + 1) * 512],
                           start=True, stop=False,
                           waits=([(s_w1[0], w1v), (s_act, max(P5_ACT, B1C_ACT, ps_h_guard[m % 2]))]
                                  if m < 2 else [(s_act, ps_h_guard[m % 2])]))
                        for k in range(1, 8):
                            EM("tensor", "matmul", hp[:],
                               w1_sb[:, k * 2048 + m * 128: k * 2048 + (m + 1) * 128],
                               xn2T[:, k * T + c * 512: k * T + (c + 1) * 512],
                               start=False, stop=(k == 7),
                               incs=([(s_pe, 1)] if k == 7 else []))
                        pv = s_pe.n
                        EM("scalar", "activation", h_sb[:, m * 512:(m + 1) * 512], hp[:],
                           AF.Relu, bias=b1c_sb[:, m * 8 + e: m * 8 + e + 1],
                           waits=[(s_pe, max(pv, h_evac_guard))], incs=[(s_act, 1)])
                        ps_h_guard[m % 2] = s_act.n
                    h_act = s_act.n
                    if c == 3:
                        h_last_pe[e + 1] = s_pe.n
                        if e < NE - 1:
                            w1_rdy[0] = load_w1(e + 1, h_last_pe[e + 1])
                    for tt in range(4):
                        i = c * 4 + tt
                        for dc in range(2):
                            vp = ps_v[dc]
                            EM("tensor", "matmul", vp[:],
                               h_sb[:, 0 * 512 + tt * 128: 0 * 512 + tt * 128 + 128],
                               w2_sb[:, 0 * 1024 + dc * 512: 0 * 1024 + (dc + 1) * 512],
                               start=True, stop=False,
                               waits=[(s_act, h_act), (s_w2, w2v), (s_dve, ps_v_guard[dc])])
                            for m in range(1, 16):
                                EM("tensor", "matmul", vp[:],
                                   h_sb[:, m * 512 + tt * 128: m * 512 + tt * 128 + 128],
                                   w2_sb[:, m * 1024 + dc * 512: m * 1024 + (dc + 1) * 512],
                                   start=False, stop=(m == 15),
                                   incs=([(s_pe, 1)] if m == 15 else []))
                            pv = s_pe.n
                            EM("vector", "scalar_tensor_tensor",
                               acc_sb[:, i * 1024 + dc * 512: i * 1024 + (dc + 1) * 512],
                               vp[:], g_sb[:, i * 8 + e: i * 8 + e + 1],
                               acc_sb[:, i * 1024 + dc * 512: i * 1024 + (dc + 1) * 512],
                               ALU.mult, ALU.add,
                               waits=[(s_pe, pv), (s_dve, GATES_DVE)], incs=[(s_dve, 1)])
                            ps_v_guard[dc] = s_dve.n
                    acc_dve = s_dve.n
                    h_evac_guard = s_pe.n
                y_last_pe[e + 1] = s_pe.n
            ACC_DVE = s_dve.n

            if debug:
                EM("gpsimd", "dma_start", out=dbg["dbg_acc"], in_=acc_sb[:],
                   waits=[(s_dve, ACC_DVE)], incs=[(s_dbg, 16)])

            xq_free = [0, 0]
            out_val = [0, 0]
            for i in range(NT):
                sl = i % 2
                xt_ap = xt_sb[:, sl * C:(sl + 1) * C]
                o_ap = out_sb[:, sl * C:(sl + 1) * C]
                EM("sync", "dma_start", out=xt_ap, in_=x_spill[i * 128:(i + 1) * 128, :],
                   waits=[(s_spill[0], SPILL0), (s_spill[1], SPILL1),
                          (s_gp, xq_free[sl]), (s_dve, xp_free[sl]),
                          (s_xp[0], XP_FIN[0]), (s_xp[1], XP_FIN[1])],
                   incs=[(s_xq[sl], 16)])
                lv = s_xq[sl].n
                for dc in range(2):
                    EM("tensor", "matmul", ps_v[dc][:],
                       gT_sb[0:8, i * 128:(i + 1) * 128],
                       b2_sb[0:8, dc * 512:(dc + 1) * 512],
                       start=True, stop=True,
                       waits=[(s_act, GATES_ACT), (s_wrb, WRB_D), (s_dve, ps_v_guard[dc])],
                       incs=[(s_pe, 1)])
                gb_pe = s_pe.n
                for dc in range(2):
                    EM("vector", "scalar_tensor_tensor",
                       o_ap[:, dc * 512:(dc + 1) * 512], ps_v[dc][:], 1.0,
                       acc_sb[:, i * 1024 + dc * 512: i * 1024 + (dc + 1) * 512],
                       ALU.mult, ALU.add,
                       waits=[(s_pe, gb_pe), (s_dve, ACC_DVE),
                              (s_out[sl], out_val[sl]), (s_gp, xq_free[sl])],
                       incs=[(s_dve, 1)])
                dv = s_dve.n
                ps_v_guard[0] = dv
                ps_v_guard[1] = dv
                EM("gpsimd", "tensor_tensor", o_ap, o_ap, xt_ap, ALU.add,
                   waits=[(s_dve, dv), (s_xq[sl], lv)], incs=[(s_gp, 1)])
                gv = s_gp.n
                xq_free[sl] = gv
                EM("sync", "dma_start", out=y_out[i * 128:(i + 1) * 128, :], in_=o_ap,
                   waits=[(s_gp, gv)], incs=[(s_out[sl], 16)])
                out_val[sl] = s_out[sl].n
            OUT0 = s_out[0].n
            OUT1 = s_out[1].n

            def _final(h, OUT0=OUT0, OUT1=OUT1, DBG_D=s_dbg.n):
                h.wait_ge(s_out[0].h, OUT0)
                h.wait_ge(s_out[1].h, OUT1)
                if DBG_D:
                    h.wait_ge(s_dbg.h, DBG_D)

            P.steps["sync"].append(_final)

        with nc.Block() as block:
            @block.sync
            def _(sync):
                for fn in P.steps["sync"]:
                    fn(sync)

            @block.scalar
            def _(scalar):
                for fn in P.steps["scalar"]:
                    fn(scalar)

            @block.vector
            def _(vector):
                for fn in P.steps["vector"]:
                    fn(vector)

            @block.tensor
            def _(tensor):
                for fn in P.steps["tensor"]:
                    fn(tensor)

            @block.gpsimd
            def _(gpsimd):
                for fn in P.steps["gpsimd"]:
                    fn(gpsimd)

    return nc


def _host_consts():
    bf = ml_dtypes.bfloat16
    eye = np.eye(128, dtype=np.float32).astype(bf)
    eye64 = np.concatenate([np.eye(64, dtype=np.float32)] * 2, axis=0).astype(bf)
    eye8 = np.eye(8, dtype=np.float32)
    u = np.arange(1024)[None, :]
    p = np.arange(128)[:, None]
    mask = np.where(u <= p + 384, np.float32(0.0), np.float32(NEGBIG)).astype(bf)
    return eye, eye64, eye8, mask


_BUILD_CACHE = {}


def _get_nc(debug=False):
    key = bool(debug)
    if key not in _BUILD_CACHE:
        _BUILD_CACHE[key] = build(debug)
    return _BUILD_CACHE[key]


def kernel(L0_attn_x, L0_attn_w_attn, L0_attn_w_proj, L0_attn_embk, L0_attn_embv,
           L0_ffn_Wr, L0_ffn_br, L0_ffn_W1, L0_ffn_b1, L0_ffn_W2, L0_ffn_b2,
           _debug=False):
    x = np.ascontiguousarray(np.asarray(L0_attn_x, dtype=np.float32))
    w_attn = np.ascontiguousarray(np.asarray(L0_attn_w_attn, dtype=np.float32))
    w_proj = np.ascontiguousarray(np.asarray(L0_attn_w_proj, dtype=np.float32))
    embk = np.asarray(L0_attn_embk, dtype=np.float32)
    embv = np.asarray(L0_attn_embv, dtype=np.float32)
    wr = np.ascontiguousarray(np.asarray(L0_ffn_Wr, dtype=np.float32))
    br = np.ascontiguousarray(np.asarray(L0_ffn_br, dtype=np.float32).reshape(1, NE))
    w1 = np.ascontiguousarray(np.asarray(L0_ffn_W1, dtype=np.float32))
    b1 = np.ascontiguousarray(np.asarray(L0_ffn_b1, dtype=np.float32))
    w2 = np.ascontiguousarray(np.asarray(L0_ffn_W2, dtype=np.float32))
    b2 = np.ascontiguousarray(np.asarray(L0_ffn_b2, dtype=np.float32))

    embkRT = np.ascontiguousarray(embk[::-1, :].T)
    embvR = np.ascontiguousarray(embv[::-1, :])
    eye, eye64, eye8, mask = _host_consts()

    nc = _get_nc(_debug)
    common = dict(w_attn=w_attn, w_proj=w_proj, embkRT=embkRT, embvR=embvR,
                  wr=wr, br=br, w1=w1, b1=b1, w2=w2, b2=b2,
                  eye_bf=eye, eye64x2_bf=eye64, eye8_f=eye8, mask_bf=mask)
    in_maps = [dict(common, x=np.ascontiguousarray(x[b])) for b in range(8)]
    res = run_bass_kernel_spmd(nc, in_maps, core_ids=list(range(8)))
    out = np.stack([np.asarray(r["y"], dtype=np.float32) for r in res.results], axis=0)
    if _debug:
        return out, res
    return out


# version 16
# speedup vs baseline: 1.0446x; 1.0019x over previous
import numpy as np
import ml_dtypes
import concourse.bass as bass
import concourse.mybir as mybir
from concourse.bass_utils import run_bass_kernel_spmd
from concourse.bass import AP
from contextlib import ExitStack

F32 = mybir.dt.float32
BF16 = mybir.dt.bfloat16
AF = mybir.ActivationFunctionType
ALU = mybir.AluOpType
AXX = mybir.AxisListType.X

T = 2048
C = 1024
NH = 16
DH = 64
NT = 16
HD = 2048
NE = 8
WS = 2048
JM = 1535
NEGBIG = -1.0e30

ENGS = ("sync", "scalar", "vector", "tensor", "gpsimd")


class Sem:
    __slots__ = ("h", "n", "name")

    def __init__(self, ctx, nc, name):
        self.h = ctx.enter_context(nc.semaphore(name))
        self.n = 0
        self.name = name


class Prog:
    def __init__(self):
        self.steps = {e: [] for e in ENGS}
        self.last_wait = {}

    def emit(self, eng, method, *args, waits=(), incs=(), **kw):
        wl = []
        for s, v in waits:
            if v <= 0:
                continue
            assert v <= s.n, f"wait on {s.name} for {v} but only {s.n} emitted"
            key = (eng, s.name)
            if self.last_wait.get(key, -1) < v:
                self.last_wait[key] = v
                wl.append((s.h, v))
        il = []
        for s, k in incs:
            s.n += k
            il.append((s.h, k))

        def fn(h, method=method, args=args, kw=kw, wl=wl, il=il):
            for sh, v in wl:
                h.wait_ge(sh, v)
            ins = getattr(h, method)(*args, **kw)
            for sh, k in il:
                ins.then_inc(sh, k)

        self.steps[eng].append(fn)


def build(debug=False):
    nc = bass.Bass("TRN2", target_bir_lowering=False, debug=False)

    x_in = nc.dram_tensor("x", [T, C], F32, kind="ExternalInput").ap()
    w_attn = nc.dram_tensor("w_attn", [C, 3 * C], F32, kind="ExternalInput").ap()
    w_proj = nc.dram_tensor("w_proj", [C, C], F32, kind="ExternalInput").ap()
    embkRT = nc.dram_tensor("embkRT", [DH, 1024], F32, kind="ExternalInput").ap()
    embvR = nc.dram_tensor("embvR", [1024, DH], F32, kind="ExternalInput").ap()
    wr_in = nc.dram_tensor("wr", [C, NE], F32, kind="ExternalInput").ap()
    br_in = nc.dram_tensor("br", [1, NE], F32, kind="ExternalInput").ap()
    w1_in = nc.dram_tensor("w1", [NE, C, HD], F32, kind="ExternalInput").ap()
    b1_in = nc.dram_tensor("b1", [NE, HD], F32, kind="ExternalInput").ap()
    w2_in = nc.dram_tensor("w2", [NE, HD, C], F32, kind="ExternalInput").ap()
    b2_in = nc.dram_tensor("b2", [NE, C], F32, kind="ExternalInput").ap()
    eye_bf = nc.dram_tensor("eye_bf", [128, 128], BF16, kind="ExternalInput").ap()
    eye64x2_bf = nc.dram_tensor("eye64x2_bf", [128, 64], BF16, kind="ExternalInput").ap()
    eye8_f = nc.dram_tensor("eye8_f", [8, 8], F32, kind="ExternalInput").ap()
    mask_bf = nc.dram_tensor("mask_bf", [128, 1024], BF16, kind="ExternalInput").ap()

    y_out = nc.dram_tensor("y", [T, C], F32, kind="ExternalOutput").ap()

    dbg = {}
    dkind = "ExternalOutput" if debug else "Internal"
    if debug:
        for nm, shp in (("dbg_xnT", [128, 8 * T]), ("dbg_qkv", [128, 6 * T]),
                        ("dbg_att", [128, T]), ("dbg_yT", [128, 8 * T]),
                        ("dbg_gates", [128, 2 * 128]), ("dbg_xn2T", [128, 8 * T]),
                        ("dbg_acc", [128, 16 * C])):
            dbg[nm] = nc.dram_tensor(nm, shp, F32, kind="ExternalOutput").ap()
    x_spill = nc.dram_tensor("x_spill", [T, C], F32, kind=dkind).ap()
    prev_h = [nc.dram_tensor(f"prevh{p}", [T, WS], BF16, kind=dkind).ap() for p in range(2)]
    bmat_h = [nc.dram_tensor(f"bmath{p}", [T, WS], BF16, kind=dkind).ap() for p in range(2)]

    P = Prog()
    EM = P.emit

    with ExitStack() as gctx:
        E = gctx.enter_context
        SM = lambda name: Sem(gctx, nc, name)

        eye_sb = E(nc.sbuf_tensor("eye_sb", [128, 128], BF16))
        eye64_sb = E(nc.sbuf_tensor("eye64_sb", [128, 64], BF16))
        eyef_sb = E(nc.sbuf_tensor("eyef_sb", [8, 8], F32))
        embkRT_sb = E(nc.sbuf_tensor("embkRT_sb", [128, 1024], BF16))
        e66_sb = E(nc.sbuf_tensor("e66_sb", [128, 8 * 66], BF16))
        ev_sb = E(nc.sbuf_tensor("ev_sb", [128, 64], BF16))
        ones_bf = E(nc.sbuf_tensor("ones_bf", [1, 128], BF16))
        ones_f = E(nc.sbuf_tensor("ones_f", [128, 64], F32))
        zero_bf = E(nc.sbuf_tensor("zero_bf", [128, 1024], BF16))
        negbig_bf = E(nc.sbuf_tensor("negbig_bf", [128, 512], BF16))
        xt_sb = E(nc.sbuf_tensor("xt_sb", [128, 2 * C], F32))
        xn_sb = E(nc.sbuf_tensor("xn_sb", [128, 2 * C], BF16))
        st_sb = E(nc.sbuf_tensor("st_sb", [128, 16], F32))
        plast_sb = E(nc.sbuf_tensor("plast_sb", [128, 16], F32))
        cst_sb = E(nc.sbuf_tensor("cst_sb", [128, 8], F32))

        s_const = SM("s_const")
        s_pe = SM("s_pe")
        s_act = SM("s_act")
        s_dve = SM("s_dve")
        s_x = [SM(f"s_x{k}") for k in range(2)]
        s_wq = [SM(f"s_wq{k}") for k in range(2)]
        s_prv = [SM(f"s_prv{k}") for k in range(4)]
        s_rr = [SM(f"s_rr{k}") for k in range(8)]
        s_bz = [SM(f"s_bz{k}") for k in range(2)]
        s_bw = [SM(f"s_bw{k}") for k in range(8)]
        s_bt = [SM(f"s_bt{k}") for k in range(8)]
        s_wp = SM("s_wp")
        s_pg = SM("s_pg")
        s_gp = SM("s_gp")
        s_spill = [SM(f"s_spill{k}") for k in range(2)]
        s_xp = [SM(f"s_xp{k}") for k in range(2)]
        s_xq = [SM(f"s_xq{k}") for k in range(2)]
        s_w1 = [SM(f"s_w1{k}") for k in range(2)]
        s_w2 = SM("s_w2")
        s_wrb = SM("s_wrb")
        s_out = [SM(f"s_out{k}") for k in range(2)]
        s_dbg = SM("s_dbg")

        EM("gpsimd", "dma_start", out=eye_sb[:], in_=eye_bf, incs=[(s_const, 16)])
        EM("gpsimd", "dma_start", out=eye64_sb[:], in_=eye64x2_bf, incs=[(s_const, 16)])
        EM("gpsimd", "dma_start", out=eyef_sb[:], in_=eye8_f, incs=[(s_const, 16)])
        EM("gpsimd", "dma_start", out=embkRT_sb[0:64, :], in_=embkRT, incs=[(s_const, 16)])
        EM("gpsimd", "dma_start", out=embkRT_sb[64:128, :], in_=embkRT, incs=[(s_const, 16)])
        for rt in range(8):
            EM("gpsimd", "dma_start", out=e66_sb[:, rt * 66:rt * 66 + 64],
               in_=embvR[rt * 128:(rt + 1) * 128, :], incs=[(s_const, 16)])
        EM("gpsimd", "dma_start", out=ev_sb[65:66, :], in_=embvR[0:1, :], incs=[(s_const, 16)])
        th = e66_sb[:].tensor
        EM("vector", "memset", AP(th, 64, [[8 * 66, 128], [66, 8], [1, 1]]), 0.0, incs=[(s_dve, 1)])
        EM("vector", "memset", AP(th, 65, [[8 * 66, 128], [66, 8], [1, 1]]), -1.0, incs=[(s_dve, 1)])
        EM("vector", "memset", ones_bf[:], 1.0, incs=[(s_dve, 1)])
        EM("vector", "memset", ones_f[:], 1.0, incs=[(s_dve, 1)])
        EM("vector", "memset", zero_bf[:], 0.0, incs=[(s_dve, 1)])
        EM("vector", "memset", ev_sb[0:65, :], 0.0, incs=[(s_dve, 1)])
        EM("vector", "memset", cst_sb[:, 0:1], 1e-5, incs=[(s_dve, 1)])
        EM("vector", "memset", cst_sb[:, 1:2], -0.5, incs=[(s_dve, 1)])
        EM("vector", "memset", cst_sb[:, 2:3], 0.125, incs=[(s_dve, 1)])
        EM("vector", "memset", cst_sb[:, 3:4], 1.0 / C, incs=[(s_dve, 1)])
        EM("vector", "memset", cst_sb[:, 4:5], -32.0, incs=[(s_dve, 1)])
        EM("vector", "memset", cst_sb[:, 5:6], 0.0, incs=[(s_dve, 1)])
        EM("vector", "memset", negbig_bf[:], NEGBIG, incs=[(s_dve, 1)])
        CONST_D = s_const.n
        MS_D = s_dve.n

        ln_prev_dve = [0]

        def emit_ln(xt_ap, xn_ap, first_waits):
            EM("scalar", "activation", xn_ap, xt_ap, AF.Copy,
               accum_out=st_sb[:, 0:1],
               waits=list(first_waits) + [(s_dve, ln_prev_dve[0])], incs=[(s_act, 1)])
            EM("scalar", "activation", xn_ap, xt_ap, AF.Square,
               accum_out=st_sb[:, 1:2], waits=[(s_act, s_act.n)], incs=[(s_act, 1)])
            EM("scalar", "activation", st_sb[:, 2:3], st_sb[:, 0:1], AF.Copy,
               scale=cst_sb[:, 3:4], waits=[(s_act, s_act.n)], incs=[(s_act, 1)])
            EM("scalar", "activation", st_sb[:, 3:4], st_sb[:, 1:2], AF.Copy,
               scale=cst_sb[:, 3:4], waits=[(s_act, s_act.n)], incs=[(s_act, 1)])
            EM("scalar", "activation", st_sb[:, 4:5], st_sb[:, 2:3], AF.Square,
               waits=[(s_act, s_act.n)], incs=[(s_act, 1)])
            a = s_act.n
            EM("vector", "tensor_tensor", st_sb[:, 5:6], st_sb[:, 3:4], st_sb[:, 4:5],
               ALU.subtract, waits=[(s_act, a)], incs=[(s_dve, 1)])
            d = s_dve.n
            EM("scalar", "activation", st_sb[:, 6:7], st_sb[:, 5:6], AF.Ln,
               bias=cst_sb[:, 0:1], waits=[(s_dve, d)], incs=[(s_act, 1)])
            EM("scalar", "activation", st_sb[:, 7:8], st_sb[:, 6:7], AF.Exp,
               scale=cst_sb[:, 1:2], waits=[(s_act, s_act.n)], incs=[(s_act, 1)])
            a = s_act.n
            EM("vector", "tensor_scalar", xn_ap, xt_ap, st_sb[:, 2:3], st_sb[:, 7:8],
               ALU.subtract, ALU.mult, waits=[(s_act, a)], incs=[(s_dve, 1)])
            ln_prev_dve[0] = s_dve.n
            return s_dve.n

        def emit_xpose(xn_tensor, xn_base, dstT_handle, t0, ps_tr, ps_tr_guard, extra_pe_waits):
            for cb in range(8):
                w = list(extra_pe_waits) + [(s_act, ps_tr_guard[0])] if cb == 0 else []
                EM("tensor", "transpose", ps_tr[:, cb * 128:(cb + 1) * 128],
                   xn_tensor[:, xn_base + cb * 128: xn_base + (cb + 1) * 128],
                   eye_sb[:], waits=w, incs=([(s_pe, 1)] if cb == 7 else []))
            pv = s_pe.n
            dst = AP(dstT_handle, t0, [[8 * T, 128], [T, 8], [1, 128]])
            EM("scalar", "copy", dst, ps_tr[:], waits=[(s_pe, pv)], incs=[(s_act, 1)])
            ps_tr_guard[0] = s_act.n
            return s_act.n

        with ExitStack() as actx:
            EA = actx.enter_context
            xnT = EA(nc.sbuf_tensor("xnT", [128, 8 * T], BF16))
            qkv_sb = EA(nc.sbuf_tensor("qkv_sb", [128, 6 * T], BF16))
            yT = EA(nc.sbuf_tensor("yT", [128, 8 * T], BF16))
            wqkv_sb = EA(nc.sbuf_tensor("wqkv_sb", [128, 6144], BF16))
            wproj_sb = EA(nc.sbuf_tensor("wproj_sb", [128, 8192], BF16))
            att_sb = EA(nc.sbuf_tensor("att_sb", [128, 4 * T], BF16))
            attT_sb = EA(nc.sbuf_tensor("attT_sb", [128, 1024], BF16))
            prev_sb = EA(nc.sbuf_tensor("prev_sb", [128, 2 * 1536], BF16))
            rel_sb = EA(nc.sbuf_tensor("rel_sb", [128, 4 * 2048], BF16))
            bt_sb = EA(nc.sbuf_tensor("bt_sb", [128, 8 * 512], BF16))
            vaug_sb = EA(nc.sbuf_tensor("vaug_sb", [128, 2 * 16 * 66], BF16))
            rep_sb = EA(nc.sbuf_tensor("rep_sb", [64, 2 * 512], F32))
            fin_sb = EA(nc.sbuf_tensor("fin_sb", [128, 1024], F32))
            finb_sb = EA(nc.sbuf_tensor("finb_sb", [128, 512], BF16))

            ps_s = [EA(nc.psum_tensor(f"ps_s{k}", [128, 512], F32)) for k in range(4)]
            ps_t = EA(nc.psum_tensor("ps_t", [128, 1024], BF16))
            ps_t2 = EA(nc.psum_tensor("ps_t2", [128, 1024], BF16))
            ps_y = [EA(nc.psum_tensor(f"ps_y{k}", [128, 512], F32)) for k in range(2)]

            vh = vaug_sb[:].tensor
            EM("vector", "memset", AP(vh, 64, [[2 * 16 * 66, 128], [66, 32], [1, 2]]), 1.0,
               incs=[(s_dve, 1)])
            MS_D2 = s_dve.n

            for p in range(2):
                for g16 in range(16):
                    EM("sync", "dma_start",
                       out=AP(prev_h[p].tensor, (g16 * 128) * WS + 1536,
                              [[WS, 128], [1, 512]]),
                       in_=negbig_bf[:],
                       waits=([(s_dve, MS_D)] if (p == 0 and g16 == 0) else []),
                       incs=[(s_pg, 16)])
            PG_D = s_pg.n

            for p in range(2):
                for i8 in range(8):
                    EM("sync", "dma_start",
                       out=AP(bmat_h[p].tensor, i8 * 128 * WS + 512,
                              [[WS, 128], [1, 1024]]),
                       in_=zero_bf[:],
                       waits=([(s_dve, MS_D)] if (p == 0 and i8 == 0) else []),
                       incs=[(s_bz[p], 16)])
            BZA = [s_bz[0].n, s_bz[1].n]

            pool_banks = ps_s
            pool_state = [[0, 0] for _ in range(4)]
            pool_ctr = [0]

            def pool_alloc():
                k = pool_ctr[0] % 4
                pool_ctr[0] += 1
                return k

            ps_t_guard = [0]
            ps_t_guard_d = [0]
            ps_t2_guard_d = [0]
            ps_s_guard = [(0, 0)] * 4
            ps_y_guard = [0, 0]
            aux_guard = [0]
            bt_guard = [0] * 8
            pending_fin = []
            pending_trav = []
            rel_guard = [0] * 8
            vaug_guard = [0, 0]
            bm_free = [0, 0]
            fin_pe_guard = [0]
            fin_war_dve = [0]
            fin_rec = [0]
            fin_fb = [0]
            plast_dve = [0]
            dbg_att_guard = [0]
            prev_free = [0, 0]
            att_tr_done = [0] * 4

            xt_free = [0, 0]
            xn_free = [0, 0]
            for i in range(NT):
                sl = i % 2
                xt_ap = xt_sb[:, sl * C:(sl + 1) * C]
                xn_base = sl * C
                xn_ap = xn_sb[:, sl * C:(sl + 1) * C]
                EM("sync", "dma_start", out=xt_ap, in_=x_in[i * 128:(i + 1) * 128, :],
                   waits=[(s_dve, xt_free[sl])], incs=[(s_x[sl], 16)])
                lv = s_x[sl].n
                xt_free[sl] = emit_ln(
                    xt_ap, xn_ap,
                    [(s_x[sl], lv), (s_const, CONST_D), (s_dve, MS_D2), (s_pe, xn_free[sl])])
                emit_xpose(xn_sb[:].tensor, xn_base, xnT[:].tensor, i * 128, ps_t,
                           ps_t_guard, [(s_dve, xt_free[sl])])
                xn_free[sl] = s_pe.n
            P1_ACT = s_act.n

            if debug:
                EM("gpsimd", "dma_start", out=dbg["dbg_xnT"], in_=xnT[:],
                   waits=[(s_act, P1_ACT)], incs=[(s_dbg, 16)])

            qkv_mm_end = [0] * 5

            def load_wqkv(g):
                gate = qkv_mm_end[g - 1] if g >= 1 else 0
                for k in range(8):
                    for j in range(3):
                        EM("gpsimd", "dma_start",
                           out=wqkv_sb[:, j * 2048 + k * 256: j * 2048 + (k + 1) * 256],
                           in_=w_attn[k * 128:(k + 1) * 128, j * C + g * 256: j * C + (g + 1) * 256],
                           waits=([(s_pe, gate)] if (k == 0 and j == 0) else []),
                           incs=[(s_wq[0], 16)])
                return s_wq[0].n

            wq_ready = [0] * 4
            wq_ready[0] = load_wqkv(0)

            for g in range(4):
                par = 0
                wbase = 0
                for jout in range(6):
                    j3, mh = jout // 2, jout % 2
                    for tc in range(4):
                        bk = pool_alloc()
                        bg = pool_state[bk]
                        EM("tensor", "matmul", pool_banks[bk][:],
                           wqkv_sb[:, wbase + j3 * 2048 + 0 * 256 + mh * 128:
                                   wbase + j3 * 2048 + 0 * 256 + mh * 128 + 128],
                           xnT[:, 0 * T + tc * 512: 0 * T + (tc + 1) * 512],
                           start=True, stop=False,
                           waits=[(s_wq[0], wq_ready[g]), (s_act, max(P1_ACT, bg[0])),
                                  (s_dve, bg[1])],
                           incs=[])
                        for k in range(1, 8):
                            EM("tensor", "matmul", pool_banks[bk][:],
                               wqkv_sb[:, wbase + j3 * 2048 + k * 256 + mh * 128:
                                       wbase + j3 * 2048 + k * 256 + mh * 128 + 128],
                               xnT[:, k * T + tc * 512: k * T + (tc + 1) * 512],
                               start=False, stop=(k == 7),
                               incs=([(s_pe, 1)] if k == 7 else []))
                        pv = s_pe.n
                        dst = qkv_sb[:, jout * T + tc * 512:jout * T + (tc + 1) * 512]
                        if j3 == 1:
                            EM("scalar", "activation", dst, pool_banks[bk][:], AF.Copy,
                               scale=cst_sb[:, 2:3],
                               waits=[(s_pe, pv)], incs=[(s_act, 1)])
                        else:
                            EM("scalar", "copy", dst, pool_banks[bk][:],
                               waits=[(s_pe, pv)], incs=[(s_act, 1)])
                        pool_state[bk][0] = s_act.n
                qkv_mm_end[g] = s_pe.n
                if g < 3:
                    wq_ready[g + 1] = load_wqkv(g + 1)
                QKV_ACT = s_act.n

                if debug and g == 0:
                    EM("gpsimd", "dma_start", out=dbg["dbg_qkv"], in_=qkv_sb[:],
                       waits=[(s_act, QKV_ACT)], incs=[(s_dbg, 16)])

                for hl in range(4):
                    h = 4 * g + hl
                    hpar = h % 2
                    qbase = (hl // 2) * T
                    poff = 64 * (hl % 2)
                    PRV = prev_h[hpar]
                    BM = bmat_h[hpar]

                    def qT(pa, pb, a, b):
                        return qkv_sb[pa:pb, 0 * T + qbase + a: 0 * T + qbase + b]

                    def kT(pa, pb, a, b):
                        return qkv_sb[pa:pb, 2 * T + qbase + a: 2 * T + qbase + b]

                    def vT(pa, pb, a, b):
                        return qkv_sb[pa:pb, 4 * T + qbase + a: 4 * T + qbase + b]

                    prv_w_val = [0, 0]
                    for i in range(NT):
                        psl = i % 2
                        cstart = max(0, 1408 - i * 128)
                        if pending_trav and i in (4, 8, 12):
                            pending_trav.pop(0)()
                        for u in range(2):
                            if u == 0 and cstart >= 1024:
                                continue
                            bk = pool_alloc()
                            bg = pool_state[bk]
                            EM("tensor", "matmul", pool_banks[bk][:],
                               qT(poff, poff + 64, i * 128, (i + 1) * 128),
                               embkRT_sb[poff:poff + 64, u * 512:(u + 1) * 512],
                               start=True, stop=True,
                               waits=[(s_act, bg[0]), (s_act, QKV_ACT),
                                      (s_const, CONST_D), (s_dve, bg[1])],
                               incs=[(s_pe, 1)])
                            pv = s_pe.n
                            if u == 0:
                                EM("vector", "tensor_copy", plast_sb[:, i:i + 1],
                                   pool_banks[bk][:, 0:1], waits=[(s_pe, pv)], incs=[(s_dve, 1)])
                                plast_dve[0] = s_dve.n
                                cs2 = max(cstart, 512)
                                EM("vector", "tensor_copy",
                                   prev_sb[:, psl * 1536 + cs2: psl * 1536 + 1024],
                                   pool_banks[bk][:, cs2 - 512: 512],
                                   waits=[(s_pe, pv), (s_prv[hpar * 2 + psl], prv_w_val[psl])],
                                   incs=[(s_dve, 1)])
                                pool_state[bk][1] = s_dve.n
                            else:
                                EM("scalar", "copy",
                                   prev_sb[:, psl * 1536 + 1024: psl * 1536 + 1536],
                                   pool_banks[bk][:],
                                   waits=[(s_pe, pv), (s_prv[hpar * 2 + psl], prv_w_val[psl])],
                                   incs=[(s_act, 1)])
                                pool_state[bk][0] = s_act.n
                        dvv = s_dve.n
                        if cstart < 512:
                            EM("scalar", "add",
                               prev_sb[:, psl * 1536 + cstart: psl * 1536 + 512],
                               zero_bf[:, 0:512 - cstart], plast_sb[:, i:i + 1],
                               waits=[(s_dve, dvv)], incs=[(s_act, 1)])
                        av = s_act.n
                        EM("sync", "dma_start",
                           out=AP(PRV.tensor, i * 128 * WS + cstart,
                                  [[WS, 128], [1, 1536 - cstart]]),
                           in_=prev_sb[:, psl * 1536 + cstart:(psl + 1) * 1536],
                           waits=([(s_act, av), (s_dve, dvv)] +
                                  ([(s_rr[hpar * 4 + q], prev_free[hpar]) for q in range(4)]
                                   if i == 0 else [])),
                           incs=[(s_prv[hpar * 2 + psl], 16)])
                        prv_w_val[psl] = s_prv[hpar * 2 + psl].n
                    PREV_D0 = s_prv[hpar * 2].n
                    PREV_D1 = s_prv[hpar * 2 + 1].n

                    strip_exp_act = [0] * NT
                    bw_val = [0] * 4
                    rel_issued = [False] * NT

                    def issue_rel(i):
                        t0 = i * 128
                        s_lo = max(0, t0 - 1408)
                        w_i = (i // 4 + 1) * 512 - s_lo
                        sl = i % 4
                        EM("sync", "dma_start",
                           out=rel_sb[:, sl * 2048: sl * 2048 + w_i],
                           in_=AP(PRV.tensor, t0 * WS + (JM - t0 + s_lo), [[WS - 1, 128], [1, w_i]]),
                           waits=[(s_prv[hpar * 2], PREV_D0), (s_prv[hpar * 2 + 1], PREV_D1),
                                  (s_dve, rel_guard[hpar * 4 + sl]), (s_pg, PG_D)],
                           incs=[(s_rr[hpar * 4 + sl], 16)])
                        rel_issued[i] = s_rr[hpar * 4 + sl].n

                    issue_rel(0)
                    issue_rel(1)
                    issue_rel(2)
                    for q4 in range(4):
                        for j in range(4):
                            st = q4 * 4 + j
                            EM("tensor", "transpose", ps_t[:, q4 * 256 + j * 64: q4 * 256 + (j + 1) * 64],
                               vT(poff, poff + 64, st * 128, (st + 1) * 128),
                               eye64_sb[poff:poff + 64, :],
                               waits=([(s_act, max(QKV_ACT, ps_t_guard[0])),
                                       (s_dve, ps_t_guard_d[0])] if j == 0 else []),
                               incs=([(s_pe, 1)] if j == 3 else []))
                        pv = s_pe.n
                        EM("scalar", "copy",
                           AP(vh, (hpar * 16 + q4 * 4) * 66, [[2 * 16 * 66, 128], [66, 4], [1, 64]]),
                           ps_t[:, q4 * 256:(q4 + 1) * 256],
                           waits=[(s_pe, max(pv, vaug_guard[hpar])), (s_dve, MS_D2)],
                           incs=[(s_act, 1)])
                        ps_t_guard[0] = s_act.n
                    VAUG_ACT = s_act.n

                    for i in range(NT):
                        gstrip = (g * 4 + hl) * NT + i
                        t0 = i * 128
                        nch = i // 4 + 1
                        s_lo = max(0, t0 - 1408)
                        ssl = i % 4
                        att_base = ssl * T
                        relv = rel_issued[i]
                        while pending_fin and pending_fin[0][0] <= gstrip:
                            pending_fin.pop(0)[1]()
                        chunk_act = [0] * nch
                        first = [True]

                        def fst():
                            if first[0]:
                                first[0] = False
                                return [(s_rr[hpar * 4 + ssl], relv), (s_const, CONST_D)]
                            return []

                        for ch in range(nch):
                            bk = pool_alloc()
                            bg = pool_state[bk]
                            EM("tensor", "matmul", pool_banks[bk][:],
                               qT(poff, poff + 64, t0, t0 + 128),
                               kT(poff, poff + 64, ch * 512, (ch + 1) * 512),
                               start=True, stop=True,
                               waits=[(s_act, bg[0]), (s_dve, bg[1])],
                               incs=[(s_pe, 1)])
                            qv = s_pe.n
                            a = max(ch * 512, s_lo)
                            b = (ch + 1) * 512
                            if b > a:
                                EM("vector", "tensor_tensor",
                                   pool_banks[bk][:, a - ch * 512: b - ch * 512],
                                   pool_banks[bk][:, a - ch * 512: b - ch * 512],
                                   rel_sb[:, ssl * 2048 + (a - s_lo): ssl * 2048 + (b - s_lo)],
                                   ALU.add, waits=fst() + [(s_pe, qv)], incs=[(s_dve, 1)])
                            if s_lo > ch * 512:
                                fb = min((ch + 1) * 512, s_lo)
                                EM("vector", "tensor_scalar",
                                   pool_banks[bk][:, 0: fb - ch * 512],
                                   pool_banks[bk][:, 0: fb - ch * 512],
                                   plast_sb[:, i:i + 1], None, ALU.add,
                                   waits=fst() + [(s_pe, qv)], incs=[(s_dve, 1)])
                            adv = s_dve.n
                            pool_state[bk][1] = adv
                            w = [(s_dve, adv)]
                            if ch == 0:
                                w += [(s_pe, att_tr_done[ssl]),
                                      (s_bw[hpar * 4 + ssl], bw_val[ssl]),
                                      (s_dbg, dbg_att_guard[0])]
                            EM("scalar", "activation",
                               att_sb[:, att_base + ch * 512: att_base + (ch + 1) * 512],
                               pool_banks[bk][:], AF.Exp, bias=cst_sb[:, 4:5], waits=w,
                               incs=[(s_act, 1)])
                            chunk_act[ch] = s_act.n
                            pool_state[bk][0] = s_act.n
                        sdv = s_dve.n
                        rel_guard[hpar * 4 + ssl] = sdv
                        if i + 3 < NT:
                            issue_rel(i + 3)
                        expv = s_act.n
                        strip_exp_act[i] = expv
                        w_i = t0 + 128 - s_lo
                        EM("sync", "dma_start",
                           out=AP(BM.tensor, t0 * WS + (JM - t0 + s_lo), [[WS - 1, 128], [1, w_i]]),
                           in_=att_sb[:, att_base + s_lo: att_base + t0 + 128],
                           waits=[(s_act, expv), (s_bz[hpar], BZA[hpar]),
                                  (s_pe, bm_free[hpar])],
                           incs=[(s_bw[hpar * 4 + ssl], 16)])
                        bw_val[ssl] = s_bw[hpar * 4 + ssl].n

                        def trav_closure(i=i, nch=nch, att_base=att_base, ssl=ssl,
                                         chunk_act=tuple(chunk_act),
                                         VAUG_ACT=VAUG_ACT, hpar=hpar):
                            ypar = (i // 4) % 2
                            nst = i + 1
                            ngr = (nst + 3) // 4
                            copy_dve = {}

                            def gspan(q4):
                                return q4 * 4, min(q4 * 4 + 4, nst)

                            def tbank(q4):
                                return ps_t if q4 % 2 == 0 else ps_t2

                            def emit_trans(q4):
                                j0, j1 = gspan(q4)
                                cov = chunk_act[min((j1 - 1) // 4, nch - 1)]
                                gd = copy_dve.get(
                                    q4 - 2,
                                    ps_t_guard_d[0] if q4 % 2 == 0 else ps_t2_guard_d[0])
                                for st in range(j0, j1):
                                    EM("tensor", "transpose",
                                       tbank(q4)[:, (st % 8) * 128: (st % 8 + 1) * 128],
                                       att_sb[:, att_base + st * 128: att_base + (st + 1) * 128],
                                       eye_sb[:],
                                       waits=([(s_act, max(cov, ps_t_guard[0])),
                                               (s_dve, gd)] if st == j0 else []),
                                       incs=([(s_pe, 1)] if st == j1 - 1 else []))
                                return s_pe.n

                            def emit_copy(q4, pv):
                                j0, j1 = gspan(q4)
                                EM("vector", "tensor_copy",
                                   attT_sb[:, (j0 % 8) * 128: (j0 % 8) * 128 + (j1 - j0) * 128],
                                   tbank(q4)[:, (j0 % 8) * 128: (j0 % 8) * 128 + (j1 - j0) * 128],
                                   waits=[(s_pe, pv)], incs=[(s_dve, 1)])
                                copy_dve[q4] = s_dve.n
                                if q4 % 2 == 0:
                                    ps_t_guard_d[0] = s_dve.n
                                else:
                                    ps_t2_guard_d[0] = s_dve.n

                            def emit_avs(q4):
                                j0, j1 = gspan(q4)
                                atv = copy_dve[q4]
                                for st in range(j0, j1):
                                    EM("tensor", "matmul",
                                       ps_y[ypar][0:66, (i % 4) * 128:(i % 4 + 1) * 128],
                                       vaug_sb[:, (hpar * 16 + st) * 66:(hpar * 16 + st) * 66 + 66],
                                       attT_sb[:, (st % 8) * 128:(st % 8 + 1) * 128],
                                       start=(i % 4 == 0 and st == 0), stop=False,
                                       skip_group_check=True,
                                       waits=([(s_dve, atv), (s_act, VAUG_ACT)] +
                                              ([(s_dve, ps_y_guard[ypar])]
                                               if (i % 4 == 0 and st == 0) else [])),
                                       incs=([(s_pe, 1)] if st == j1 - 1 else []))

                            emit_copy(0, emit_trans(0))
                            for q4 in range(1, ngr):
                                pv = emit_trans(q4)
                                emit_avs(q4 - 1)
                                emit_copy(q4, pv)
                            emit_avs(ngr - 1)
                            att_tr_done[ssl] = s_pe.n
                            vaug_guard[hpar] = s_pe.n

                        pending_trav.append(trav_closure)
                        if len(pending_trav) > 3:
                            pending_trav.pop(0)()

                        if debug and h == 0 and i == NT - 1:
                            EM("gpsimd", "dma_start", out=dbg["dbg_att"],
                               in_=att_sb[:, att_base:att_base + T],
                               waits=[(s_act, expv)], incs=[(s_dbg, 16)])
                            dbg_att_guard[0] = s_dbg.n

                        if i % 4 == 3:
                            tc = i // 4
                            ypar = (i // 4) % 2
                            bww = [(s_bw[hpar * 4 + q], s_bw[hpar * 4 + q].n) for q in range(4)]
                            btvals = []
                            for rt in range(8):
                                EM("sync", "dma_start",
                                   out=bt_sb[:, rt * 512:(rt + 1) * 512],
                                   in_=BM[tc * 512:(tc + 1) * 512, 512 + rt * 128: 640 + rt * 128],
                                   transpose=True,
                                   waits=bww + [(s_pe, bt_guard[rt]),
                                                (s_bz[hpar], BZA[hpar])],
                                   incs=[(s_bt[rt], 16)])
                                btvals.append(s_bt[rt].n)

                            def fin_closure(tc=tc, ypar=ypar, h=h, hpar=hpar,
                                            btvals=tuple(btvals)):
                                for rt in range(8):
                                    EM("tensor", "matmul", ps_y[ypar][0:66, :],
                                       e66_sb[:, rt * 66:(rt + 1) * 66],
                                       bt_sb[:, rt * 512:(rt + 1) * 512],
                                       start=False, stop=(rt == 7), skip_group_check=True,
                                       waits=[(s_bt[rt], btvals[rt]), (s_dve, MS_D)],
                                       incs=[(s_pe, 1)])
                                    bt_guard[rt] = s_pe.n
                                yrel_pe = s_pe.n
                                bm_free[hpar] = yrel_pe
                                EM("scalar", "copy", fin_sb[64:66, 0:512],
                                   ps_y[ypar][64:66, :],
                                   waits=[(s_pe, max(yrel_pe, fin_pe_guard[0])),
                                          (s_dve, fin_war_dve[0])],
                                   incs=[(s_act, 1)])
                                fav = s_act.n
                                EM("scalar", "copy", finb_sb[64:66, 0:512],
                                   ps_y[ypar][64:66, :], incs=[(s_act, 1)])
                                finb_av = s_act.n
                                EM("vector", "reciprocal", fin_sb[64:65, 512:1024],
                                   fin_sb[64:65, 0:512], waits=[(s_act, fav)],
                                   incs=[(s_dve, 1)])
                                fin_war_dve[0] = s_dve.n
                                fin_rec[0] = s_dve.n
                                fin_fb[0] = finb_av

                            def fin_closure_b(tc=tc, ypar=ypar, h=h,
                                              fin_rec=fin_rec, fin_fb=fin_fb):
                                EM("tensor", "matmul", ps_y[ypar][0:64, :],
                                   ev_sb[64:66, :], finb_sb[64:66, 0:512],
                                   start=False, stop=True, skip_group_check=True,
                                   waits=[(s_act, fin_fb[0]), (s_const, CONST_D)],
                                   incs=[(s_pe, 1)])
                                rbk = pool_alloc()
                                rbg = pool_state[rbk]
                                EM("tensor", "matmul", pool_banks[rbk][0:64, :],
                                   ones_f[64:65, :], fin_sb[64:65, 512:1024],
                                   start=True, stop=True,
                                   waits=[(s_act, rbg[0]), (s_dve, fin_rec[0]),
                                          (s_dve, rbg[1])],
                                   incs=[(s_pe, 1)])
                                fin_pe_guard[0] = s_pe.n
                                rep_pe = s_pe.n
                                EM("scalar", "copy",
                                   rep_sb[0:64, (tc % 2) * 512:(tc % 2 + 1) * 512],
                                   pool_banks[rbk][0:64, :], waits=[(s_pe, rep_pe)],
                                   incs=[(s_act, 1)])
                                pool_state[rbk][0] = s_act.n
                                rav = s_act.n
                                EM("vector", "tensor_tensor",
                                   yT[64 * (h % 2):64 * (h % 2) + 64,
                                      (h // 2) * T + tc * 512: (h // 2) * T + (tc + 1) * 512],
                                   ps_y[ypar][0:64, :],
                                   rep_sb[0:64, (tc % 2) * 512:(tc % 2 + 1) * 512],
                                   ALU.mult, waits=[(s_act, rav)], incs=[(s_dve, 1)])
                                ps_y_guard[ypar] = s_dve.n

                            pending_fin.append((gstrip + 4, fin_closure))
                            pending_fin.append((gstrip + 6, fin_closure_b))

                    prev_free[hpar] = max(s_rr[hpar * 4 + q].n for q in range(4))

            while pending_trav:
                pending_trav.pop(0)()
            while pending_fin:
                pending_fin.pop(0)[1]()
            YT_DVE = s_dve.n
            if debug:
                EM("gpsimd", "dma_start", out=dbg["dbg_yT"], in_=yT[:],
                   waits=[(s_dve, YT_DVE)], incs=[(s_dbg, 16)])

            for k in range(8):
                EM("gpsimd", "dma_start", out=wproj_sb[:, k * 1024:(k + 1) * 1024],
                   in_=w_proj[k * 128:(k + 1) * 128, :], incs=[(s_wp, 16)])
            WP_D = s_wp.n

            spill_val = [0, 0]
            for i in range(NT):
                sl = i % 2
                xt_ap = xt_sb[:, sl * C:(sl + 1) * C]
                EM("sync", "dma_start", out=xt_ap, in_=x_in[i * 128:(i + 1) * 128, :],
                   waits=[(s_spill[sl], spill_val[sl]), (s_dve, xt_free[sl])],
                   incs=[(s_x[sl], 16)])
                lv = s_x[sl].n
                for dc in range(2):
                    bk = pool_alloc()
                    psb = pool_banks[bk]
                    bg = pool_state[bk]
                    EM("tensor", "matmul", psb[:],
                       yT[:, 0 * T + i * 128: 0 * T + i * 128 + 128],
                       wproj_sb[:, 0 * 1024 + dc * 512: 0 * 1024 + (dc + 1) * 512],
                       start=True, stop=False,
                       waits=[(s_dve, max(YT_DVE, bg[1])), (s_act, bg[0]), (s_wp, WP_D)])
                    for k in range(1, 8):
                        EM("tensor", "matmul", psb[:],
                           yT[:, k * T + i * 128: k * T + i * 128 + 128],
                           wproj_sb[:, k * 1024 + dc * 512: k * 1024 + (dc + 1) * 512],
                           start=False, stop=(k == 7),
                           incs=([(s_pe, 1)] if k == 7 else []))
                    pv = s_pe.n
                    EM("vector", "tensor_tensor", xt_ap[:, dc * 512:(dc + 1) * 512],
                       psb[:], xt_ap[:, dc * 512:(dc + 1) * 512], ALU.add,
                       waits=[(s_pe, pv), (s_x[sl], lv)], incs=[(s_dve, 1)])
                    pool_state[bk][1] = s_dve.n
                dv = s_dve.n
                EM("sync", "dma_start", out=x_spill[i * 128:(i + 1) * 128, :], in_=xt_ap,
                   waits=[(s_dve, dv)], incs=[(s_spill[sl], 16)])
                spill_val[sl] = s_spill[sl].n
                xt_free[sl] = dv
            SPILL0 = s_spill[0].n
            SPILL1 = s_spill[1].n
            P4_PE = s_pe.n
            P4_DVE = s_dve.n
            P4_ACT = s_act.n

        with ExitStack() as fctx:
            EF = fctx.enter_context
            xn2T = EF(nc.sbuf_tensor("xn2T", [128, 8 * T], BF16))
            w1_sb = EF(nc.sbuf_tensor("w1_sb", [128, 8 * 2048], BF16))
            w2_sb = EF(nc.sbuf_tensor("w2_sb", [128, 16 * 1024], BF16))
            h_sb = EF(nc.sbuf_tensor("h_sb", [128, 16 * 512], BF16))
            acc_sb = EF(nc.sbuf_tensor("acc_sb", [128, 16 * 1024], BF16))
            wr_sb = EF(nc.sbuf_tensor("wr_sb", [128, 64], BF16))
            br_sb = EF(nc.sbuf_tensor("br_sb", [1, 8], BF16))
            b1c_sb = EF(nc.sbuf_tensor("b1c_sb", [128, 128], F32))
            rep_br = EF(nc.sbuf_tensor("rep_br", [128, 8], F32))
            b2_sb = EF(nc.sbuf_tensor("b2_sb", [8, 1024], BF16))
            p8_sb = EF(nc.sbuf_tensor("p8_sb", [128, 8], F32))
            g_sb = EF(nc.sbuf_tensor("g_sb", [128, 128], F32))
            gbf_sb = EF(nc.sbuf_tensor("gbf_sb", [128, 8], BF16))
            out_sb = EF(nc.sbuf_tensor("out_sb", [128, 2 * C], F32))
            gT_sb = EF(nc.sbuf_tensor("gT_sb", [8, 2048], BF16))

            ps_a = EF(nc.psum_tensor("ps_a", [128, 1024], BF16))
            ps_h = [EF(nc.psum_tensor(f"ps_h{k}", [128, 512], F32)) for k in range(2)]
            ps_v = [EF(nc.psum_tensor(f"ps_v{k}", [128, 512], F32)) for k in range(2)]
            ps_r = EF(nc.psum_tensor("ps_r", [128, 512], F32))

            BAR = [(s_pe, P4_PE), (s_dve, P4_DVE), (s_act, P4_ACT),
                   (s_spill[0], SPILL0), (s_spill[1], SPILL1),
                   (s_dbg, s_dbg.n)]

            for k in range(8):
                EM("gpsimd", "dma_start", out=wr_sb[:, k * 8:(k + 1) * 8],
                   in_=wr_in[k * 128:(k + 1) * 128, :],
                   waits=(BAR if k == 0 else []), incs=[(s_wrb, 16)])
            EM("gpsimd", "dma_start", out=br_sb[:], in_=br_in, incs=[(s_wrb, 16)])
            b1_8 = acc_sb[0:8, 0:4096].bitcast(F32)
            EM("gpsimd", "dma_start", out=b1_8, in_=b1_in,
               waits=BAR, incs=[(s_wrb, 16)])
            EM("gpsimd", "dma_start", out=b2_sb[:], in_=b2_in, incs=[(s_wrb, 16)])
            WRB_D = s_wrb.n
            for m in range(16):
                EM("tensor", "transpose", ps_r[0:128, 0:8],
                   b1_8[:, m * 128:(m + 1) * 128], eyef_sb[:],
                   waits=[(s_wrb, WRB_D)] + (BAR if m == 0 else []) +
                         ([(s_act, s_act.n)] if m > 0 else []),
                   incs=[(s_pe, 1)])
                pv = s_pe.n
                EM("scalar", "copy", b1c_sb[:, m * 8:(m + 1) * 8], ps_r[0:128, 0:8],
                   waits=[(s_pe, pv)], incs=[(s_act, 1)])
            EM("tensor", "matmul", ps_r[0:128, 0:8],
               ones_bf[0:1, :], br_sb[:], start=True, stop=True,
               waits=[(s_wrb, WRB_D), (s_act, s_act.n)], incs=[(s_pe, 1)])
            EM("scalar", "copy", rep_br[:], ps_r[0:128, 0:8],
               waits=[(s_pe, s_pe.n)], incs=[(s_act, 1)])
            B1C_ACT = s_act.n
            EM("vector", "memset", acc_sb[:], 0.0, waits=BAR + [(s_pe, s_pe.n)],
               incs=[(s_dve, 1)])
            ACCZ = s_dve.n

            def load_w1(e, gate_pe):
                for k in range(8):
                    EM("gpsimd", "dma_start",
                       out=w1_sb[:, k * 2048:(k + 1) * 2048],
                       in_=w1_in[e, k * 128:(k + 1) * 128, :],
                       waits=([(s_pe, gate_pe)] if k == 0 else []),
                       incs=[(s_w1[0], 16)])
                return s_w1[0].n

            def load_w2(e, gate_pe):
                for m in range(16):
                    EM("gpsimd", "dma_start",
                       out=w2_sb[:, m * 1024:(m + 1) * 1024],
                       in_=w2_in[e, m * 128:(m + 1) * 128, :],
                       waits=([(s_pe, gate_pe)] if m == 0 else []),
                       incs=[(s_w2, 16)])
                return s_w2.n

            ps_a_guard = [0]
            xp_free = [0, 0]
            xn_free2 = [0, 0]
            for i in range(NT):
                sl = i % 2
                xt_ap = xt_sb[:, sl * C:(sl + 1) * C]
                xn_base = sl * C
                xn_ap = xn_sb[:, sl * C:(sl + 1) * C]
                EM("sync", "dma_start", out=xt_ap, in_=x_spill[i * 128:(i + 1) * 128, :],
                   waits=[(s_spill[0], SPILL0), (s_spill[1], SPILL1),
                          (s_dve, xp_free[sl])], incs=[(s_xp[sl], 16)])
                lv = s_xp[sl].n
                xp_free[sl] = emit_ln(xt_ap, xn_ap, [(s_xp[sl], lv), (s_pe, xn_free2[sl])] + BAR)
                emit_xpose(xn_sb[:].tensor, xn_base, xn2T[:].tensor, i * 128, ps_a,
                           ps_a_guard, [(s_dve, xp_free[sl])] + (BAR if i < 2 else []))
                xn_free2[sl] = s_pe.n
            XP_FIN = [s_xp[0].n, s_xp[1].n]
            P5_ACT = s_act.n

            if debug:
                EM("gpsimd", "dma_start", out=dbg["dbg_xn2T"], in_=xn2T[:],
                   waits=[(s_act, P5_ACT)], incs=[(s_dbg, 16)])

            r_guard_act = B1C_ACT
            r_guard_dve = 0
            r_guard_gp = 0
            for i in range(NT):
                EM("tensor", "matmul", ps_r[0:128, 0:8],
                   xn2T[:, 0 * T + i * 128: 0 * T + i * 128 + 128], wr_sb[:, 0:8],
                   start=True, stop=False,
                   waits=[(s_act, max(P5_ACT, r_guard_act)), (s_wrb, WRB_D),
                          (s_dve, r_guard_dve)])
                for k in range(1, 8):
                    EM("tensor", "matmul", ps_r[0:128, 0:8],
                       xn2T[:, k * T + i * 128: k * T + i * 128 + 128],
                       wr_sb[:, k * 8:(k + 1) * 8], start=False, stop=(k == 7),
                       incs=([(s_pe, 1)] if k == 7 else []))
                pv = s_pe.n
                EM("vector", "tensor_tensor", ps_r[0:128, 0:8], ps_r[0:128, 0:8],
                   rep_br[:], ALU.add,
                   waits=[(s_pe, pv), (s_dve, r_guard_dve), (s_act, B1C_ACT),
                          (s_gp, r_guard_gp)],
                   incs=[(s_dve, 1)])
                dv = s_dve.n
                EM("scalar", "activation", p8_sb[:], ps_r[0:128, 0:8],
                   AF.Exp, accum_out=st_sb[:, 1:2],
                   waits=[(s_dve, dv)], incs=[(s_act, 1)])
                av = s_act.n
                r_guard_act = av
                EM("vector", "reciprocal", st_sb[:, 2:3], st_sb[:, 1:2],
                   waits=[(s_act, av)], incs=[(s_dve, 1)])
                EM("vector", "reduce_max", st_sb[:, 3:4], p8_sb[:], axis=AXX,
                   incs=[(s_dve, 1)])
                dv = s_dve.n
                EM("gpsimd", "tensor_tensor", st_sb[:, 4:5], st_sb[:, 3:4], st_sb[:, 2:3],
                   ALU.mult, waits=[(s_dve, dv)], incs=[(s_gp, 1)])
                gv = s_gp.n
                EM("vector", "tensor_scalar", g_sb[:, i * 8:(i + 1) * 8],
                   p8_sb[:], st_sb[:, 3:4], st_sb[:, 4:5],
                   ALU.is_ge, ALU.mult, waits=[(s_gp, gv)], incs=[(s_dve, 1)])
                dv = s_dve.n
                EM("gpsimd", "tensor_copy", gbf_sb[:], g_sb[:, i * 8:(i + 1) * 8],
                   waits=[(s_dve, dv)], incs=[(s_gp, 1)])
                r_guard_dve = s_dve.n
                r_guard_gp = s_gp.n
                EM("tensor", "transpose", ps_a[0:8, 0:128], gbf_sb[:], eye_sb[:],
                   waits=[(s_gp, r_guard_gp), (s_act, ps_a_guard[0])], incs=[(s_pe, 1)])
                pv = s_pe.n
                EM("scalar", "copy", gT_sb[0:8, i * 128:(i + 1) * 128], ps_a[0:8, 0:128],
                   waits=[(s_pe, pv)], incs=[(s_act, 1)])
                ps_a_guard[0] = s_act.n
                r_guard_act = s_act.n
            GATES_ACT = s_act.n
            GATES_DVE = s_dve.n

            if debug:
                EM("gpsimd", "dma_start", out=dbg["dbg_gates"][:, 0:128], in_=g_sb[:],
                   waits=[(s_dve, GATES_DVE)], incs=[(s_dbg, 16)])
                EM("gpsimd", "dma_start", out=dbg["dbg_gates"][:, 128:136], in_=p8_sb[:],
                   waits=[(s_dve, GATES_DVE)], incs=[(s_dbg, 16)])

            w1_rdy = [0, 0]
            w1_rdy[0] = load_w1(0, 0)
            h_last_pe = [0] * (NE + 1)
            y_last_pe = [0] * (NE + 1)
            h_evac_guard = 0
            ps_h_guard = [0, 0]
            ps_v_guard = [ACCZ, ACCZ]
            acc_dve = ACCZ
            for e in range(NE):
                par = 0
                w1v = w1_rdy[0]
                w2v = load_w2(e, y_last_pe[e])
                for c in range(4):
                    for m in range(16):
                        hp = ps_h[m % 2]
                        EM("tensor", "matmul", hp[:],
                           w1_sb[:, 0 * 2048 + m * 128: 0 * 2048 + (m + 1) * 128],
                           xn2T[:, 0 * T + c * 512: 0 * T + (c + 1) * 512],
                           start=True, stop=False,
                           waits=([(s_w1[0], w1v), (s_act, max(P5_ACT, B1C_ACT, ps_h_guard[m % 2]))]
                                  if m < 2 else [(s_act, ps_h_guard[m % 2])]))
                        for k in range(1, 8):
                            EM("tensor", "matmul", hp[:],
                               w1_sb[:, k * 2048 + m * 128: k * 2048 + (m + 1) * 128],
                               xn2T[:, k * T + c * 512: k * T + (c + 1) * 512],
                               start=False, stop=(k == 7),
                               incs=([(s_pe, 1)] if k == 7 else []))
                        pv = s_pe.n
                        EM("scalar", "activation", h_sb[:, m * 512:(m + 1) * 512], hp[:],
                           AF.Relu, bias=b1c_sb[:, m * 8 + e: m * 8 + e + 1],
                           waits=[(s_pe, max(pv, h_evac_guard))], incs=[(s_act, 1)])
                        ps_h_guard[m % 2] = s_act.n
                    h_act = s_act.n
                    if c == 3:
                        h_last_pe[e + 1] = s_pe.n
                        if e < NE - 1:
                            w1_rdy[0] = load_w1(e + 1, h_last_pe[e + 1])
                    for tt in range(4):
                        i = c * 4 + tt
                        for dc in range(2):
                            vp = ps_v[dc]
                            EM("tensor", "matmul", vp[:],
                               h_sb[:, 0 * 512 + tt * 128: 0 * 512 + tt * 128 + 128],
                               w2_sb[:, 0 * 1024 + dc * 512: 0 * 1024 + (dc + 1) * 512],
                               start=True, stop=False,
                               waits=[(s_act, h_act), (s_w2, w2v), (s_dve, ps_v_guard[dc])])
                            for m in range(1, 16):
                                EM("tensor", "matmul", vp[:],
                                   h_sb[:, m * 512 + tt * 128: m * 512 + tt * 128 + 128],
                                   w2_sb[:, m * 1024 + dc * 512: m * 1024 + (dc + 1) * 512],
                                   start=False, stop=(m == 15),
                                   incs=([(s_pe, 1)] if m == 15 else []))
                            pv = s_pe.n
                            EM("vector", "scalar_tensor_tensor",
                               acc_sb[:, i * 1024 + dc * 512: i * 1024 + (dc + 1) * 512],
                               vp[:], g_sb[:, i * 8 + e: i * 8 + e + 1],
                               acc_sb[:, i * 1024 + dc * 512: i * 1024 + (dc + 1) * 512],
                               ALU.mult, ALU.add,
                               waits=[(s_pe, pv), (s_dve, GATES_DVE)], incs=[(s_dve, 1)])
                            ps_v_guard[dc] = s_dve.n
                    acc_dve = s_dve.n
                    h_evac_guard = s_pe.n
                y_last_pe[e + 1] = s_pe.n
            ACC_DVE = s_dve.n

            if debug:
                EM("gpsimd", "dma_start", out=dbg["dbg_acc"], in_=acc_sb[:],
                   waits=[(s_dve, ACC_DVE)], incs=[(s_dbg, 16)])

            xq_free = [0, 0]
            out_val = [0, 0]
            for i in range(NT):
                sl = i % 2
                xt_ap = xt_sb[:, sl * C:(sl + 1) * C]
                o_ap = out_sb[:, sl * C:(sl + 1) * C]
                EM("sync", "dma_start", out=xt_ap, in_=x_spill[i * 128:(i + 1) * 128, :],
                   waits=[(s_spill[0], SPILL0), (s_spill[1], SPILL1),
                          (s_gp, xq_free[sl]), (s_dve, xp_free[sl]),
                          (s_xp[0], XP_FIN[0]), (s_xp[1], XP_FIN[1])],
                   incs=[(s_xq[sl], 16)])
                lv = s_xq[sl].n
                for dc in range(2):
                    EM("tensor", "matmul", ps_v[dc][:],
                       gT_sb[0:8, i * 128:(i + 1) * 128],
                       b2_sb[0:8, dc * 512:(dc + 1) * 512],
                       start=True, stop=True,
                       waits=[(s_act, GATES_ACT), (s_wrb, WRB_D), (s_dve, ps_v_guard[dc])],
                       incs=[(s_pe, 1)])
                gb_pe = s_pe.n
                for dc in range(2):
                    EM("vector", "scalar_tensor_tensor",
                       o_ap[:, dc * 512:(dc + 1) * 512], ps_v[dc][:], 1.0,
                       acc_sb[:, i * 1024 + dc * 512: i * 1024 + (dc + 1) * 512],
                       ALU.mult, ALU.add,
                       waits=[(s_pe, gb_pe), (s_dve, ACC_DVE),
                              (s_out[sl], out_val[sl]), (s_gp, xq_free[sl])],
                       incs=[(s_dve, 1)])
                dv = s_dve.n
                ps_v_guard[0] = dv
                ps_v_guard[1] = dv
                EM("gpsimd", "tensor_tensor", o_ap, o_ap, xt_ap, ALU.add,
                   waits=[(s_dve, dv), (s_xq[sl], lv)], incs=[(s_gp, 1)])
                gv = s_gp.n
                xq_free[sl] = gv
                EM("sync", "dma_start", out=y_out[i * 128:(i + 1) * 128, :], in_=o_ap,
                   waits=[(s_gp, gv)], incs=[(s_out[sl], 16)])
                out_val[sl] = s_out[sl].n
            OUT0 = s_out[0].n
            OUT1 = s_out[1].n

            def _final(h, OUT0=OUT0, OUT1=OUT1, DBG_D=s_dbg.n):
                h.wait_ge(s_out[0].h, OUT0)
                h.wait_ge(s_out[1].h, OUT1)
                if DBG_D:
                    h.wait_ge(s_dbg.h, DBG_D)

            P.steps["sync"].append(_final)

        with nc.Block() as block:
            @block.sync
            def _(sync):
                for fn in P.steps["sync"]:
                    fn(sync)

            @block.scalar
            def _(scalar):
                for fn in P.steps["scalar"]:
                    fn(scalar)

            @block.vector
            def _(vector):
                for fn in P.steps["vector"]:
                    fn(vector)

            @block.tensor
            def _(tensor):
                for fn in P.steps["tensor"]:
                    fn(tensor)

            @block.gpsimd
            def _(gpsimd):
                for fn in P.steps["gpsimd"]:
                    fn(gpsimd)

    return nc


def _host_consts():
    bf = ml_dtypes.bfloat16
    eye = np.eye(128, dtype=np.float32).astype(bf)
    eye64 = np.concatenate([np.eye(64, dtype=np.float32)] * 2, axis=0).astype(bf)
    eye8 = np.eye(8, dtype=np.float32)
    u = np.arange(1024)[None, :]
    p = np.arange(128)[:, None]
    mask = np.where(u <= p + 384, np.float32(0.0), np.float32(NEGBIG)).astype(bf)
    return eye, eye64, eye8, mask


_BUILD_CACHE = {}


def _get_nc(debug=False):
    key = bool(debug)
    if key not in _BUILD_CACHE:
        _BUILD_CACHE[key] = build(debug)
    return _BUILD_CACHE[key]


def kernel(L0_attn_x, L0_attn_w_attn, L0_attn_w_proj, L0_attn_embk, L0_attn_embv,
           L0_ffn_Wr, L0_ffn_br, L0_ffn_W1, L0_ffn_b1, L0_ffn_W2, L0_ffn_b2,
           _debug=False):
    x = np.ascontiguousarray(np.asarray(L0_attn_x, dtype=np.float32))
    w_attn = np.ascontiguousarray(np.asarray(L0_attn_w_attn, dtype=np.float32))
    w_proj = np.ascontiguousarray(np.asarray(L0_attn_w_proj, dtype=np.float32))
    embk = np.asarray(L0_attn_embk, dtype=np.float32)
    embv = np.asarray(L0_attn_embv, dtype=np.float32)
    wr = np.ascontiguousarray(np.asarray(L0_ffn_Wr, dtype=np.float32))
    br = np.ascontiguousarray(np.asarray(L0_ffn_br, dtype=np.float32).reshape(1, NE))
    w1 = np.ascontiguousarray(np.asarray(L0_ffn_W1, dtype=np.float32))
    b1 = np.ascontiguousarray(np.asarray(L0_ffn_b1, dtype=np.float32))
    w2 = np.ascontiguousarray(np.asarray(L0_ffn_W2, dtype=np.float32))
    b2 = np.ascontiguousarray(np.asarray(L0_ffn_b2, dtype=np.float32))

    embkRT = np.ascontiguousarray(embk[::-1, :].T)
    embvR = np.ascontiguousarray(embv[::-1, :])
    eye, eye64, eye8, mask = _host_consts()

    nc = _get_nc(_debug)
    common = dict(w_attn=w_attn, w_proj=w_proj, embkRT=embkRT, embvR=embvR,
                  wr=wr, br=br, w1=w1, b1=b1, w2=w2, b2=b2,
                  eye_bf=eye, eye64x2_bf=eye64, eye8_f=eye8, mask_bf=mask)
    in_maps = [dict(common, x=np.ascontiguousarray(x[b])) for b in range(8)]
    res = run_bass_kernel_spmd(nc, in_maps, core_ids=list(range(8)))
    out = np.stack([np.asarray(r["y"], dtype=np.float32) for r in res.results], axis=0)
    if _debug:
        return out, res
    return out
